# version 13
# baseline (speedup 1.0000x reference)
import contextlib
import numpy as np
import ml_dtypes
import concourse.bass as bass
import concourse.mybir as mybir
import concourse.bass_utils as _bu
from concourse.bass_utils import run_bass_kernel_spmd

if not getattr(_bu, "_ldw_patch", False):
    _orig_run_command = _bu.run_command

    def _run_command_ldw(cmd, *a, **kw):
        try:
            cmd = [c.replace("--enable-ldw-opt=false", "--enable-ldw-opt=true")
                   if isinstance(c, str) else c for c in cmd]
        except Exception:
            pass
        return _orig_run_command(cmd, *a, **kw)

    _bu.run_command = _run_command_ldw
    _bu._ldw_patch = True

F32 = mybir.dt.float32
BF16 = mybir.dt.bfloat16
AF = mybir.ActivationFunctionType
ALU = mybir.AluOpType
AX = mybir.AxisListType.X

NCORES = 8
T = 4096
C = 2048
HFF = 8192
TILE = 512
NT = T // TILE
NTC = TILE // 128
NCC = C // 128
CQ = 0.501953125
INV_CQ = 1.0 / CQ
LN_EPS = 1e-5
RMS_EPS = float(np.finfo(np.float32).eps)
DEN_EPS = 1e-15

PHASE_ALL = ("p1", "cc", "qnt", "p2a", "p2b")


def build(phases=PHASE_ALL, debug=()):
    nc = bass.Bass("TRN2", target_bir_lowering=False, debug=False)

    X = nc.dram_tensor("x", [T, C], F32, kind="ExternalInput")
    WQ = nc.dram_tensor("wq", [C, C], F32, kind="ExternalInput")
    WK = nc.dram_tensor("wk", [C, C], F32, kind="ExternalInput")
    WV = nc.dram_tensor("wv", [C, C], F32, kind="ExternalInput")
    WPROJ = nc.dram_tensor("wproj", [C, C], F32, kind="ExternalInput")
    BPROJ = nc.dram_tensor("bproj", [1, C], F32, kind="ExternalInput")
    WGU = nc.dram_tensor("wgu", [C, 2048], F32, kind="ExternalInput")
    WP = nc.dram_tensor("wp", [HFF, 256], F32, kind="ExternalInput")
    EYEB = nc.dram_tensor("eyeb", [128, 128], BF16, kind="ExternalInput")
    G2IN = nc.dram_tensor("g2in", [2, 128], BF16, kind="ExternalInput")
    GIN = nc.dram_tensor("gin", [128, 2], BF16, kind="ExternalInput")
    Y = nc.dram_tensor("y", [T, C], F32, kind="ExternalOutput")

    QSC = nc.dram_tensor("q_sc", [C, T], BF16)
    X1SC = nc.dram_tensor("x1_sc", [T, C], BF16)
    WGU_PART = 16 * 128 * NCC * 128
    WP_PART = 2 * 128 * 64 * 128
    SHARE = WGU_PART + WP_PART
    QPART = nc.dram_tensor("qpart", [SHARE], BF16)
    QALL = nc.dram_tensor("qall", [8 * SHARE], BF16)

    def qall_wgu(hc):
        rank, loc = hc // 16, hc % 16
        off = rank * SHARE + loc * (128 * NCC * 128)
        return QALL.ap()[off:off + 128 * NCC * 128]\
            .rearrange("(p c k) -> p c k", p=128, c=NCC)

    def qall_wp(co2):
        rank, loc = co2 // 2, co2 % 2
        off = rank * SHARE + WGU_PART + loc * (128 * 64 * 128)
        return QALL.ap()[off:off + 128 * 64 * 128]\
            .rearrange("(p c k) -> p c k", p=128, c=64)
    CCIN = nc.dram_tensor("cc_in", [128, 2176], F32)
    CCOUT = nc.dram_tensor("cc_out", [128, 2176], F32)

    dbg = {}
    for name, shape, dt in debug:
        dbg[name] = nc.dram_tensor(name, shape, dt, kind="ExternalOutput")

    st = contextlib.ExitStack()
    with st:
        E = st.enter_context
        eye_bf = E(nc.sbuf_tensor("eye_bf", [128, 128], BF16))
        vkbd_bf = E(nc.sbuf_tensor("vkbd_bf", [128, 2176], BF16))
        g2_bf = E(nc.sbuf_tensor("g2_bf", [2, 128], BF16))
        g_bf = E(nc.sbuf_tensor("g_bf", [128, 2], BF16))
        onesc_bf = E(nc.sbuf_tensor("onesc_bf", [128, 1], BF16))
        onesr_f32 = E(nc.sbuf_tensor("onesr_f32", [1, 128], F32))
        PS = [E(nc.psum_tensor(f"ps{i}", [128, 512], F32)) for i in range(8)]

        semnames = [
            "sx0_0", "sx0_1", "sx0_2", "sx0_3", "sLN", "snm", "sva", "sv1",
            "ssq", "stp", "stpe",
            "swq_0", "swq_1", "swq_2", "swq_3", "sqmm", "sqev",
            "sqst_0", "sqst_1", "swkv_0", "swkv_1", "skmm", "svmm",
            "skev", "svev", "svkg", "svka",
            "ccd", "ccs", "ccq",
            "qwl_0", "qwl_1", "qas", "qmm", "qsr", "qbc", "qsg", "qmk",
            "qw1", "qwh", "qst_0", "qst_1", "wst_0", "wst_1",
            "aql_0", "aql_1", "ax0_0", "ax0_1", "aoe", "adv", "adc",
            "atp", "atpe", "apj", "aep", "aden", "arw", "ast_0", "ast_1", "awp",
            "ami", "axp",
            "snm2", "sva2", "sv12", "ssq2",
            "bx1_0", "bx1_1", "bx1_2", "bx1_3", "btp", "btpe",
            "bwg_0", "bwg_1", "bgm", "bum", "bsg", "bhm", "bpq", "bhq",
            "bwp_0", "bwp_1", "bfm", "bz1", "bzt", "bzte", "brw", "biv",
            "bep", "yst_0", "yst_1", "bxn", "bvr", "bvb",

            "dbgs", "p0c", "sinit",
        ]
        Z = {n: E(nc.semaphore(n)) for n in semnames}

        def psbf(i, lo=0):
            return PS[i][:, 64 * lo:64 * (lo + 1)].bitcast(BF16)

        with nc.Block() as blk:
            @blk.sync
            def _(sy):
                sy.dma_start(eye_bf[:], EYEB.ap()).then_inc(Z["p0c"], 16)
                sy.dma_start(g2_bf[:], G2IN.ap()).then_inc(Z["p0c"], 16)
                sy.dma_start(g_bf[:], GIN.ap()).then_inc(Z["p0c"], 16)
                sy.wait_ge(Z["p0c"], 48)
            @blk.vector
            def _(v):
                v.memset(onesc_bf[:], 1.0)
                v.memset(onesr_f32[:], 1.0)
                v.memset(vkbd_bf[:], 0.0)

        p1cc = contextlib.ExitStack()
        if "p1" in phases or "cc" in phases:
            vk_sb = p1cc.enter_context(nc.sbuf_tensor("vk_sb", [128, 2176], F32))
            with nc.Block() as blk:
                @blk.vector
                def _(v):
                    v.memset(vk_sb[:], 0.0)
        if "p1" in phases:
            with contextlib.ExitStack() as p1:
                E1 = p1.enter_context
                x0b = [E1(nc.sbuf_tensor(f"x0b{i}", [128, C], F32)) for i in range(4)]
                xn1tm = [E1(nc.sbuf_tensor(f"xn1tm{i}", [128, C], BF16)) for i in range(2)]
                xn1fm = E1(nc.sbuf_tensor("xn1fm", [128, NCC * 512], BF16))
                dump = [E1(nc.sbuf_tensor(f"dump{i}", [128, C], F32)) for i in range(2)]
                stat = E1(nc.sbuf_tensor("stat", [128, 32], F32))
                wqsl = [E1(nc.sbuf_tensor(f"wqsl{i}", [128, C], BF16)) for i in range(4)]
                wksl = [E1(nc.sbuf_tensor(f"wksl{i}", [128, NCC * 512], BF16)) for i in range(2)]
                wvsl = [E1(nc.sbuf_tensor(f"wvsl{i}", [128, NCC * 512], BF16)) for i in range(2)]
                ksb = [E1(nc.sbuf_tensor(f"ksb{i}", [128, 512], BF16)) for i in range(2)]
                vpad = [E1(nc.sbuf_tensor(f"vpad{i}", [128, 544], BF16)) for i in range(2)]
                qob = [E1(nc.sbuf_tensor(f"qob{i}", [128, 512], BF16)) for i in range(2)]

                with nc.Block() as blk:
                    @blk.vector
                    def _(v):
                        v.memset(vpad[0][:], 0.0)
                        v.memset(vpad[1][:], 0.0).then_inc(Z["sinit"], 1)
                        v.wait_ge(Z["sinit"], 1)
                        for i in range(2):
                            for j in range(4):
                                for a in range(4):
                                    col = 136 * j + 34 * a + 32
                                    v.memset(vpad[i][:, col:col + 1], 1.0)

                with nc.Block() as blk:
                    @blk.sync
                    def _(sy):
                        for t in range(NT + 1):
                            if t < NT:
                                for c in range(NTC):
                                    g = NTC * t + c
                                    if g >= 4:
                                        sy.wait_ge(Z["sLN"], g - 3)
                                    sy.dma_start(x0b[g % 4][:],
                                                 X.ap()[128 * g:128 * (g + 1), :])\
                                        .then_inc(Z[f"sx0_{g % 4}"], 16)
                            if t >= 1:
                                tq = t - 1
                                for co in range(16):
                                    gq = 16 * tq + co
                                    sy.wait_ge(Z["sqev"], gq + 1)
                                    sy.dma_start(
                                        QSC.ap()[128 * co:128 * (co + 1), 512 * tq:512 * (tq + 1)],
                                        qob[gq % 2][:]).then_inc(Z[f"sqst_{gq % 2}"], 16)
                        sy.wait_ge(Z["sqst_0"], 16 * 8 * NT)
                        sy.wait_ge(Z["sqst_1"], 16 * 8 * NT)

                    @blk.gpsimd
                    def _(gp):
                        def kv_load(gk):
                            jj = gk % 4
                            if gk >= 2:
                                gp.wait_ge(Z["skmm"], 4 * (gk - 1))
                                gp.wait_ge(Z["svmm"], 4 * (gk - 1))
                            for (W_, dst) in ((WK, wksl), (WV, wvsl)):
                                src = W_.ap().rearrange("(c p) k -> p c k", c=NCC)[:, :, 512 * jj:512 * (jj + 1)]
                                gp.dma_start(dst[gk % 2][:].rearrange("p (c k) -> p c k", c=NCC),
                                             src).then_inc(Z[f"swkv_{gk % 2}"], 16)
                        for t in range(NT):
                            kv_load(4 * t + 0)
                            kv_load(4 * t + 1)
                            for co in range(16):
                                gq = 16 * t + co
                                if gq >= 4:
                                    gp.wait_ge(Z["sqmm"], gq - 3)
                                src = WQ.ap().rearrange("(c p) k -> p c k", c=NCC)[:, :, 128 * co:128 * (co + 1)]
                                gp.dma_start(wqsl[gq % 4][:].rearrange("p (c k) -> p c k", c=NCC),
                                             src).then_inc(Z[f"swq_{gq % 4}"], 16)
                            kv_load(4 * t + 2)
                            kv_load(4 * t + 3)

                    @blk.vector
                    def _(v):
                        for t in range(NT):
                            for c in range(NTC):
                                g = NTC * t + c
                                s = g % 4
                                v.wait_ge(Z[f"sx0_{g % 4}"], 16 * (g // 4 + 1))
                                if g >= 4:
                                    v.wait_ge(Z["sva"], g - 3)
                                v.reduce_sum(stat[:, 8 * s:8 * s + 1], x0b[g % 4][:], axis=AX)
                            v.drain()
                            for c in range(NTC):
                                g = NTC * t + c
                                s = g % 4
                                v.tensor_scalar(stat[:, 8 * s:8 * s + 1], stat[:, 8 * s:8 * s + 1],
                                                -1.0 / C, None, ALU.mult).then_inc(Z["snm"], 1)
                            for c in range(NTC):
                                g = NTC * t + c
                                s = g % 4
                                v.wait_ge(Z["sva"], g + 1)
                                if g >= 4:
                                    v.wait_ge(Z["ssq"], g - 3)
                                v.tensor_scalar(stat[:, 8 * s + 2:8 * s + 3], stat[:, 8 * s + 1:8 * s + 2],
                                                1.0 / C, LN_EPS, ALU.mult, ALU.add).then_inc(Z["sv1"], 1)
                            for c in range(NTC):
                                g = NTC * t + c
                                s = g % 4
                                v.wait_ge(Z["ssq"], g + 1)
                                v.reciprocal(stat[:, 8 * s + 4:8 * s + 5], stat[:, 8 * s + 3:8 * s + 4])
                            v.drain()
                            for c in range(NTC):
                                g = NTC * t + c
                                s = g % 4
                                if g >= 2:
                                    v.wait_ge(Z["stp"], 16 * (g - 1))
                                v.tensor_scalar(xn1tm[g % 2][:], x0b[g % 4][:],
                                                stat[:, 8 * s:8 * s + 1], stat[:, 8 * s + 4:8 * s + 5],
                                                ALU.add, ALU.mult).then_inc(Z["sLN"], 1)
                                for cc in range(NCC):
                                    gt = 16 * g + cc
                                    v.wait_ge(Z["stp"], gt + 1)
                                    v.tensor_copy(xn1fm[:, 512 * cc + 128 * c:512 * cc + 128 * (c + 1)],
                                                  psbf(gt % 4)).then_inc(Z["stpe"], 1)
                            for jj in range(4):
                                gk = 4 * t + jj
                                for tc in range(NTC):
                                    idx = 4 * gk + tc
                                    v.wait_ge(Z["svmm"], idx + 1)
                                    if idx >= 2:
                                        v.wait_ge(Z["svkg"], idx - 1)
                                    pv = PS[1 if tc % 2 == 0 else 3]
                                    src = pv[:, 0:512].rearrange("p (j a e) -> p j a e", j=4, a=4)
                                    dstv = vpad[tc % 2][:].rearrange("p (j x) -> p j x", j=4)[:, :, 0:136]\
                                        .rearrange("p j (a e) -> p j a e", a=4)[:, :, :, 0:32]
                                    v.tensor_copy(dstv, src).then_inc(Z["svev"], 1)
                                v.wait_ge(Z["svkg"], 4 * (gk + 1))
                                for j in range(4):
                                    col = 136 * (4 * jj + j)
                                    tt = v.tensor_tensor(vk_sb[:, col:col + 136], vk_sb[:, col:col + 136],
                                                         PS[4 + j][:, 0:136], ALU.add)
                                tt.then_inc(Z["svka"], 1)

                    @blk.scalar
                    def _(sc):
                        for t in range(NT):
                            for c in range(NTC):
                                g = NTC * t + c
                                s = g % 4
                                sc.wait_ge(Z["snm"], g + 1)
                                if g >= 4:
                                    sc.wait_ge(Z["sLN"], g - 3)
                                if g >= 2:
                                    sc.wait_ge(Z["sva"], g - 1)
                                sc.activation(dump[g % 2][:], x0b[g % 4][:], AF.Square,
                                              bias=stat[:, 8 * s:8 * s + 1],
                                              accum_out=stat[:, 8 * s + 1:8 * s + 2])\
                                    .then_inc(Z["sva"], 1)
                            for c in range(NTC):
                                g = NTC * t + c
                                s = g % 4
                                sc.wait_ge(Z["sv1"], g + 1)
                                sc.activation(stat[:, 8 * s + 3:8 * s + 4], stat[:, 8 * s + 2:8 * s + 3],
                                              AF.Sqrt).then_inc(Z["ssq"], 1)
                            for co in range(16):
                                gq = 16 * t + co
                                sc.wait_ge(Z["sqmm"], gq + 1)
                                if gq >= 2:
                                    sc.wait_ge(Z[f"sqst_{gq % 2}"], 16 * (gq // 2))
                                sc.activation(qob[gq % 2][:], PS[gq % 2][0:128, 0:512], AF.Relu)\
                                    .then_inc(Z["sqev"], 1)
                            for jj in range(4):
                                gk = 4 * t + jj
                                for tc in range(NTC):
                                    idx = 4 * gk + tc
                                    sc.wait_ge(Z["skmm"], idx + 1)
                                    if idx >= 2:
                                        sc.wait_ge(Z["svkg"], idx - 1)
                                    pk = PS[0 if tc % 2 == 0 else 2]
                                    sc.activation(ksb[tc % 2][:], pk[0:128, 0:512], AF.Relu)\
                                        .then_inc(Z["skev"], 1)

                    @blk.tensor
                    def _(te):
                        def emit_vk(idx):
                            gk, tc = idx // 4, idx % 4
                            jj = gk % 4
                            te.wait_ge(Z["skev"], idx + 1)
                            te.wait_ge(Z["svev"], idx + 1)
                            if tc == 0:
                                te.wait_ge(Z["svka"], gk)
                            for j in range(4):
                                m = te.matmul(PS[4 + j][:, 0:136],
                                              ksb[tc % 2][:, 128 * j:128 * (j + 1)],
                                              vpad[tc % 2][:, 136 * j:136 * j + 136],
                                              start=(tc == 0), stop=(tc == 3))
                            m.then_inc(Z["svkg"], 1)

                        for t in range(NT):
                            for c in range(NTC):
                                g = NTC * t + c
                                for cc in range(NCC):
                                    gt = 16 * g + cc
                                    if cc == 0:
                                        te.wait_ge(Z["sLN"], g + 1)
                                    if gt >= 4:
                                        te.wait_ge(Z["stpe"], gt - 3)
                                    if t > 0 and c == 0 and cc == 0:
                                        te.wait_ge(Z["skev"], 16 * t)
                                        te.wait_ge(Z["svev"], 16 * t)
                                        te.wait_ge(Z["sqev"], 16 * t)
                                    te.transpose(psbf(gt % 4),
                                                 xn1tm[g % 2][:, 128 * cc:128 * (cc + 1)],
                                                 eye_bf[:]).then_inc(Z["stp"], 1)
                            for co in range(16):
                                gq = 16 * t + co
                                te.wait_ge(Z[f"swq_{gq % 4}"], 16 * (gq // 4 + 1))
                                if gq >= 2:
                                    te.wait_ge(Z["sqev"], gq - 1)
                                if co == 0:
                                    te.wait_ge(Z["stpe"], 64 * (t + 1))
                                for ci in range(NCC):
                                    m = te.matmul(PS[gq % 2][0:128, 0:512],
                                                  wqsl[gq % 4][:, 128 * ci:128 * (ci + 1)],
                                                  xn1fm[:, 512 * ci:512 * (ci + 1)],
                                                  start=(ci == 0), stop=(ci == NCC - 1))
                                m.then_inc(Z["sqmm"], 1)
                            pend = None
                            for jj in range(4):
                                gk = 4 * t + jj
                                for tc in range(NTC):
                                    idx = 4 * gk + tc
                                    if tc == 0:
                                        te.wait_ge(Z[f"swkv_{gk % 2}"], 32 * (gk // 2 + 1))
                                    if jj == 0 and tc == 0:
                                        te.wait_ge(Z["sqev"], 16 * (t + 1))
                                    if idx >= 2:
                                        te.wait_ge(Z["skev"], idx - 1)
                                    pk = PS[0 if tc % 2 == 0 else 2]
                                    for ci in range(NCC):
                                        m = te.matmul(pk[0:128, 0:512],
                                                      xn1fm[:, 512 * ci + 128 * tc:512 * ci + 128 * (tc + 1)],
                                                      wksl[gk % 2][:, 512 * ci:512 * (ci + 1)],
                                                      start=(ci == 0), stop=(ci == NCC - 1))
                                    m.then_inc(Z["skmm"], 1)
                                    if pend is not None:
                                        emit_vk(pend)
                                    if idx >= 2:
                                        te.wait_ge(Z["svev"], idx - 1)
                                    pv = PS[1 if tc % 2 == 0 else 3]
                                    for ci in range(NCC):
                                        m = te.matmul(pv[0:128, 0:512],
                                                      xn1fm[:, 512 * ci + 128 * tc:512 * ci + 128 * (tc + 1)],
                                                      wvsl[gk % 2][:, 512 * ci:512 * (ci + 1)],
                                                      start=(ci == 0), stop=(ci == NCC - 1))
                                    m.then_inc(Z["svmm"], 1)
                                    pend = idx
                            emit_vk(pend)

        if "cc" in phases:
            with nc.Block() as blk:
                @blk.gpsimd
                def _(gp):
                    gp.dma_start(CCIN.ap(), vk_sb[:]).then_inc(Z["ccd"], 16)
                    gp.wait_ge(Z["ccd"], 16)
                    gp.collective_compute(
                        "AllReduce", ALU.add,
                        replica_groups=[[0, 1], [2, 3], [4, 5], [6, 7]],
                        ins=[CCIN.ap().opt()], outs=[CCOUT.ap().opt()],
                    ).then_inc(Z["ccs"], 1)
                    gp.wait_ge(Z["ccs"], 1)
                    gp.dma_start(vk_sb[:], CCOUT.ap()).then_inc(Z["ccd"], 16)
                    gp.wait_ge(Z["ccd"], 32)
                    for a in range(4):
                        src = vk_sb[32 * a:32 * (a + 1), :]\
                            .rearrange("p (j e) -> p j e", j=16)[:, :, 34 * a:34 * (a + 1)]
                        dst = vkbd_bf[32 * a:32 * (a + 1), :]\
                            .rearrange("p (j e) -> p j e", j=16)[:, :, 34 * a:34 * (a + 1)]
                        gp.dma_start(dst, src).then_inc(Z["ccd"], 16)
                    gp.wait_ge(Z["ccd"], 96)
        p1cc.close()


        es_wp = contextlib.ExitStack()
        if "p2a" in phases:
            wproj_sb = es_wp.enter_context(nc.sbuf_tensor("wproj_sb", [128, NCC * 2048], BF16))
            brow_bf = es_wp.enter_context(nc.sbuf_tensor("brow_bf", [1, C], BF16))

        if "qnt" in phases:
            bigs = [("wgu", i) for i in range(NCC)] + [("wp", i) for i in range(16)]
            NBIG = len(bigs)
            NSUB = 4 * NBIG

            def sub_w(k):
                return 512 if bigs[k // 4][0] == "wgu" else 256

            with contextlib.ExitStack() as qs:
                EQ = qs.enter_context
                if "p2a" in phases:
                    wp_st = [EQ(nc.sbuf_tensor(f"q_wpst{i}", [128, 2048], F32)) for i in range(2)]
                    brow_st = EQ(nc.sbuf_tensor("q_browst", [1, C], F32))
                wbf = [EQ(nc.sbuf_tensor(f"q_wbf{i}", [128, 2048], BF16)) for i in range(2)]
                aab = [EQ(nc.sbuf_tensor(f"q_aab{i}", [128, 512], BF16)) for i in range(4)]
                scr = [EQ(nc.sbuf_tensor(f"q_scr{i}", [2, 512], BF16)) for i in range(2)]
                scf = [EQ(nc.sbuf_tensor(f"q_scf{i}", [2, 512], F32)) for i in range(2)]
                msk = [EQ(nc.sbuf_tensor(f"q_msk{i}", [128, 512], F32)) for i in range(2)]
                w1b = [EQ(nc.sbuf_tensor(f"q_w1b{i}", [128, 512], F32)) for i in range(2)]
                sgnb = [EQ(nc.sbuf_tensor(f"q_sgn{i}", [128, 512], F32)) for i in range(4)]
                whb = [EQ(nc.sbuf_tensor(f"q_whb{i}", [128, 2048], BF16)) for i in range(2)]

                def q_src(B):
                    kind, i = bigs[B]
                    if kind == "wgu":
                        return WGU.ap()[128 * i:128 * (i + 1), :]
                    return WP.ap()[512 * i:512 * (i + 1), :]\
                        .rearrange("(c p) k -> p c k", c=4)

                def q_wbf_dst(B):
                    kind, i = bigs[B]
                    if kind == "wgu":
                        return wbf[B % 2][:]
                    return wbf[B % 2][:, 0:1024].rearrange("p (c k) -> p c k", c=4)

                def q_store_pairs(B):
                    kind, i = bigs[B]
                    out = []
                    if kind == "wgu":
                        full = QPART.ap()[0:WGU_PART]\
                            .rearrange("(a p c k) -> a p c k", a=16, p=128, c=NCC)
                        for h in range(2):
                            dst = full[8 * h:8 * (h + 1), :, i, :].transpose([1, 0, 2])
                            srcv = whb[B % 2][:, 1024 * h:1024 * (h + 1)]\
                                .rearrange("p (c k) -> p c k", c=8)
                            out.append((dst, srcv))
                    else:
                        full = QPART.ap()[WGU_PART:WGU_PART + WP_PART]\
                            .rearrange("(a p c k) -> a p c k", a=2, p=128, c=64)
                        for h in range(2):
                            dst = full[h, :, 4 * i:4 * (i + 1), :]
                            srcv = whb[B % 2][:, 0:1024]\
                                .rearrange("p (c a k) -> p c a k", c=4, a=2)[:, :, h, :]
                            out.append((dst, srcv))
                    return out

                with nc.Block() as blk:
                    if "p2a" in phases:
                        @blk.sync
                        def _(sy):
                            sy.dma_start(brow_st[:], BPROJ.ap()).then_inc(Z["wst_0"], 16)
                            sy.wait_ge(Z["wst_0"], 16)
                            wsrc = WPROJ.ap().rearrange("(c p) k -> c p k", c=NCC)
                            for cch in range(NCC):
                                if cch >= 2:
                                    sy.wait_ge(Z["awp"], cch)
                                sy.dma_start(wp_st[cch % 2][:], wsrc[cch])\
                                    .then_inc(Z[f"wst_{cch % 2}"], 16)

                    @blk.gpsimd
                    def _(gp):
                        for B in range(NBIG + 2):
                            if B < NBIG:
                                if B >= 2:
                                    gp.wait_ge(Z["qas"], 4 * (B - 1))
                                    gp.wait_ge(Z["qsg"], 4 * (B - 1))
                                gp.dma_start(q_wbf_dst(B), q_src(B)).then_inc(Z[f"qwl_{B % 2}"], 16)
                            if B >= 2:
                                Bw = B - 2
                                wB = 512 if bigs[Bw][0] == "wgu" else 256
                                for s4 in range(4):
                                    j = 4 * Bw + s4
                                    gp.wait_ge(Z["qw1"], j + 1)
                                    gp.wait_ge(Z["qsg"], j + 1)
                                    if s4 == 0 and Bw >= 2:
                                        gp.wait_ge(Z[f"qst_{Bw % 2}"], 32 * (Bw // 2))
                                    gp.tensor_tensor(whb[Bw % 2][:, wB * s4:wB * (s4 + 1)],
                                                     w1b[j % 2][:, 0:wB], sgnb[j % 4][:, 0:wB],
                                                     ALU.mult).then_inc(Z["qwh"], 1)
                                gp.wait_ge(Z["qwh"], 4 * (Bw + 1))
                                for dst, srcv in q_store_pairs(Bw):
                                    gp.dma_start(dst, srcv).then_inc(Z[f"qst_{Bw % 2}"], 16)
                        gp.wait_ge(Z["qst_0"], 32 * ((NBIG + 1) // 2))
                        gp.wait_ge(Z["qst_1"], 32 * (NBIG // 2))

                    @blk.vector
                    def _(v):
                        for k in range(NSUB + 2):
                            if 1 <= k <= NSUB:
                                j = k - 1
                                wj = sub_w(j)
                                v.wait_ge(Z["qmm"], j + 1)
                                if j >= 2:
                                    v.wait_ge(Z["qbc"], j - 1)
                                    v.wait_ge(Z["qsr"], 2 * j - 2)
                                v.tensor_copy(scf[j % 2][:, 0:wj], PS[j % 2][0:2, 0:wj])\
                                    .then_inc(Z["qsr"], 1)
                                v.wait_ge(Z["qsr"], 2 * j + 1)
                                v.tensor_scalar(scr[j % 2][:, 0:wj], scf[j % 2][:, 0:wj],
                                                1.0 / 64.0, 1e-8, ALU.mult, ALU.max)\
                                    .then_inc(Z["qsr"], 1)
                            if k >= 2:
                                j = k - 2
                                wj = sub_w(j)
                                v.wait_ge(Z["qbc"], j + 1)
                                if j >= 2:
                                    v.wait_ge(Z["qw1"], j - 1)
                                v.scalar_tensor_tensor(msk[j % 2][:, 0:wj], aab[j % 4][:, 0:wj],
                                                       INV_CQ, PS[2 + j % 2][0:128, 0:wj],
                                                       ALU.mult, ALU.is_gt).then_inc(Z["qmk"], 1)
                                v.wait_ge(Z["qmk"], j + 1)
                                if j >= 2:
                                    v.wait_ge(Z["qwh"], j - 1)
                                v.tensor_tensor(w1b[j % 2][:, 0:wj], msk[j % 2][:, 0:wj],
                                                PS[2 + j % 2][0:128, 0:wj], ALU.mult)\
                                    .then_inc(Z["qw1"], 1)

                    @blk.tensor
                    def _(te):
                        for k in range(NSUB + 2):
                            if 1 <= k <= NSUB:
                                j = k - 1
                                wj = sub_w(j)
                                te.wait_ge(Z["qas"], j + 1)
                                if j >= 2:
                                    te.wait_ge(Z["qsr"], 2 * j - 2)
                                te.matmul(PS[j % 2][0:2, 0:wj], g_bf[:], aab[j % 4][:, 0:wj],
                                          start=True, stop=True).then_inc(Z["qmm"], 1)
                            if k >= 2:
                                j = k - 2
                                wj = sub_w(j)
                                te.wait_ge(Z["qsr"], 2 * (j + 1))
                                if j >= 2:
                                    te.wait_ge(Z["qw1"], j - 1)
                                te.matmul(PS[2 + j % 2][0:128, 0:wj], g2_bf[:], scr[j % 2][:, 0:wj],
                                          start=True, stop=True).then_inc(Z["qbc"], 1)

                    @blk.scalar
                    def _(sc):
                        if "p2a" in phases:
                            sc.wait_ge(Z["wst_0"], 16)
                            sc.activation(brow_bf[:], brow_st[:], AF.Copy).then_inc(Z["awp"], 1)
                        for k in range(NSUB):
                            if "p2a" in phases and k % 8 == 0 and k // 8 < NCC:
                                cch = k // 8
                                if cch % 2 == 0:
                                    sc.wait_ge(Z["wst_0"], 16 * (cch // 2 + 2))
                                else:
                                    sc.wait_ge(Z["wst_1"], 16 * ((cch + 1) // 2))
                                sc.activation(wproj_sb[:, 2048 * cch:2048 * (cch + 1)],
                                              wp_st[cch % 2][:], AF.Copy).then_inc(Z["awp"], 1)
                            B = k // 4
                            wk = sub_w(k)
                            woff = wk * (k % 4)
                            sc.wait_ge(Z[f"qwl_{B % 2}"], 16 * (B // 2 + 1))
                            if k >= 4:
                                sc.wait_ge(Z["qwh"], k - 3)
                                sc.wait_ge(Z["qmm"], k - 3)
                                sc.wait_ge(Z["qmk"], k - 3)
                            sc.activation(aab[k % 4][:, 0:wk],
                                          wbf[B % 2][:, woff:woff + wk],
                                          AF.Abs).then_inc(Z["qas"], 1)
                            sc.activation(sgnb[k % 4][:, 0:wk],
                                          wbf[B % 2][:, woff:woff + wk],
                                          AF.Sign).then_inc(Z["qsg"], 1)

        if "p2a" in phases:
            with contextlib.ExitStack() as pa:
                EA = pa.enter_context
                onesr_bf = EA(nc.sbuf_tensor("onesr_bf", [1, 128], BF16))
                q_sb = [EA(nc.sbuf_tensor(f"a_qsb{i}", [128, NCC * 512], BF16)) for i in range(2)]
                x0b2 = [EA(nc.sbuf_tensor(f"a_x0b{i}", [128, C], F32)) for i in range(2)]
                recb = [EA(nc.sbuf_tensor(f"a_rec{i}", [128, 64], F32)) for i in range(2)]
                radd = EA(nc.sbuf_tensor("a_radd", [128, 64], F32))
                attn_tm = [EA(nc.sbuf_tensor(f"a_atm{i}", [128, C], BF16)) for i in range(2)]
                attn_fm = [EA(nc.sbuf_tensor(f"a_afm{i}", [128, C], BF16)) for i in range(2)]
                araw = [EA(nc.sbuf_tensor(f"a_raw{i}", [128, 2176], BF16)) for i in range(2)]
                x0p = [EA(nc.sbuf_tensor(f"a_x0p{i}", [128, C], F32)) for i in range(2)]
                dump2 = [EA(nc.sbuf_tensor(f"a_dmp{i}", [128, C], F32)) for i in range(2)]
                stat2 = EA(nc.sbuf_tensor("a_stat", [128, 64], F32))
                x1bf = [EA(nc.sbuf_tensor(f"a_x1b{i}", [128, C], BF16)) for i in range(2)]

                with nc.Block() as blk:
                    @blk.vector
                    def _(v):
                        v.memset(onesr_bf[:], 1.0)
                    @blk.gpsimd
                    def _(gp):
                        gp.collective_compute(
                            "AllGather", ALU.bypass,
                            replica_groups=[[0, 1, 2, 3, 4, 5, 6, 7]],
                            ins=[QPART.ap().rearrange("(r x) -> r x", r=128).opt()],
                            outs=[QALL.ap().rearrange("(r x) -> r x", r=1024).opt()],
                        ).then_inc(Z["ccq"], 1)
                        for g in range(NT * NTC):
                            s = g % 4
                            gp.wait_ge(Z["ami"], g + 1)
                            gp.wait_ge(Z[f"ax0_{g % 2}"], 16 * (g // 2 + 1))
                            if g >= 2:
                                gp.wait_ge(Z["aep"], g - 1)
                            gp.tensor_scalar(x0p[g % 2][:], x0b2[g % 2][:],
                                             stat2[:, 8 * s + 5:8 * s + 6], None, ALU.add)\
                                .then_inc(Z["axp"], 1)

                    @blk.sync
                    def _(sy):
                        for t in range(NT + 1):
                            if t < NT:
                                if t >= 2:
                                    sy.wait_ge(Z["aoe"], 64 * (t - 1))
                                sy.dma_start(q_sb[t % 2][:].rearrange("p (c k) -> p c k", c=NCC),
                                             QSC.ap().rearrange("(c p) k -> p c k", c=NCC)
                                             [:, :, 512 * t:512 * (t + 1)])\
                                    .then_inc(Z[f"aql_{t % 2}"], 16)
                            if t >= 1:
                                for tc in range(NTC):
                                    g = NTC * (t - 1) + tc
                                    sy.wait_ge(Z["aep"], g + 1)
                                    sy.dma_start(X1SC.ap()[128 * g:128 * (g + 1), :],
                                                 x1bf[g % 2][:]).then_inc(Z[f"ast_{g % 2}"], 16)
                            if t < NT:
                                for tc in range(NTC):
                                    g = NTC * t + tc
                                    if g >= 2:
                                        sy.wait_ge(Z["aep"], g - 1)
                                    sy.dma_start(x0b2[g % 2][:], X.ap()[128 * g:128 * (g + 1), :])\
                                        .then_inc(Z[f"ax0_{g % 2}"], 16)
                        sy.wait_ge(Z["ast_0"], 16 * 2 * NT)
                        sy.wait_ge(Z["ast_1"], 16 * 2 * NT)

                    @blk.tensor
                    def _(te):
                        te.wait_ge(Z["awp"], 17)
                        NG = NT * NTC

                        def emit_oe(g):
                            t = g // NTC
                            tc = g % NTC
                            for j in range(16):
                                o = 16 * g + j
                                if j == 0 and tc == 0:
                                    te.wait_ge(Z[f"aql_{t % 2}"], 16 * (t // 2 + 1))
                                if j == 0 and g >= 2:
                                    te.wait_ge(Z["atpe"], 16 * (g - 1))
                                if o >= 3:
                                    te.wait_ge(Z["arw"], o - 2)
                                if j == 0 and g >= 1:
                                    te.wait_ge(Z["adc"], g)
                                qchunk = q_sb[t % 2][:, 512 * j + 128 * tc:512 * j + 128 * (tc + 1)]
                                te.matmul(PS[o % 3][0:128, 0:136], qchunk,
                                          vkbd_bf[:, 136 * j:136 * (j + 1)],
                                          start=True, stop=True).then_inc(Z["aoe"], 1)
                                dencols = vkbd_bf[:, 136 * j:136 * (j + 1)]\
                                    .rearrange("p (a e) -> p a e", a=4)[:, :, 32:33]\
                                    .rearrange("p a e -> p (a e)")
                                dm = te.matmul(PS[3][0:128, 4 * j:4 * (j + 1)], qchunk,
                                               dencols, start=True, stop=True)
                            dm.then_inc(Z["aden"], 1)

                        def emit_tail(g):
                            for cc in range(NCC):
                                gt = 16 * g + cc
                                if cc == 0:
                                    te.wait_ge(Z["adv"], 16 * (g + 1))
                                    te.wait_ge(Z["arw"], 16 * min(g + 2, NG))
                                if gt >= 3:
                                    te.wait_ge(Z["atpe"], gt - 2)
                                te.transpose(psbf(gt % 3),
                                             attn_tm[g % 2][:, 128 * cc:128 * (cc + 1)],
                                             eye_bf[:]).then_inc(Z["atp"], 1)
                            te.wait_ge(Z["atpe"], 16 * (g + 1))
                            if g >= 1:
                                te.wait_ge(Z["aep"], g)
                                te.wait_ge(Z["sva2"], g)
                            for ci in range(NCC):
                                for n in range(4):
                                    te.matmul(
                                        PS[4 + n][0:128, 0:512],
                                        attn_fm[g % 2][:, 128 * ci:128 * (ci + 1)],
                                        wproj_sb[:, 2048 * ci + 512 * n:2048 * ci + 512 * (n + 1)],
                                        start=(ci == 0), stop=False)
                            for n in range(4):
                                m = te.matmul(PS[4 + n][0:128, 0:512], onesr_bf[0:1, :],
                                              brow_bf[0:1, 512 * n:512 * (n + 1)],
                                              start=False, stop=True)
                            m.then_inc(Z["apj"], 1)

                        for g in range(NG + 1):
                            if g < NG:
                                emit_oe(g)
                            if g >= 1:
                                emit_tail(g - 1)

                    @blk.vector
                    def _(v):
                        NG = NT * NTC

                        def emit_div(g):
                            t = g // NTC
                            for j in range(16):
                                o = 16 * g + j
                                v.wait_ge(Z["aoe"], o + 1)
                                v.tensor_copy(araw[g % 2][:, 136 * j:136 * (j + 1)],
                                              PS[o % 3][0:128, 0:136]).then_inc(Z["arw"], 1)
                            v.wait_ge(Z["aden"], g + 1)
                            v.tensor_scalar(radd[:], PS[3][0:128, 0:64],
                                            DEN_EPS, None, ALU.add)
                            v.drain()
                            if g >= 2:
                                v.wait_ge(Z["adv"], 16 * (g - 1))
                            v.reciprocal(recb[g % 2][:], radd[:]).then_inc(Z["adc"], 1)

                        def emit_epi(g):
                            t = g // NTC
                            tc = g % NTC
                            s = g % 4
                            if g >= 2:
                                v.wait_ge(Z["apj"], g - 1)
                            for cc in range(NCC):
                                gt = 16 * g + cc
                                v.wait_ge(Z["atp"], gt + 1)
                                v.tensor_copy(attn_fm[g % 2][:, 128 * cc:128 * (cc + 1)],
                                              psbf(gt % 3)).then_inc(Z["atpe"], 1)
                            v.wait_ge(Z["apj"], g + 1)
                            for n in range(4):
                                v.reduce_sum(stat2[:, 8 * s + n:8 * s + n + 1],
                                             PS[4 + n][0:128, 0:512], axis=AX)
                            v.drain()
                            if g >= 4:
                                v.wait_ge(Z["sva2"], g - 3)
                            v.reduce_sum(stat2[:, 8 * s + 4:8 * s + 5],
                                         stat2[:, 8 * s:8 * s + 4], axis=AX)
                            v.drain()
                            v.tensor_scalar(stat2[:, 8 * s + 4:8 * s + 5],
                                            stat2[:, 8 * s + 4:8 * s + 5],
                                            -1.0 / C, None, ALU.mult).then_inc(Z["snm2"], 1)
                            v.wait_ge(Z["sva2"], g + 1)
                            v.reduce_sum(stat2[:, 8 * s + 5:8 * s + 6],
                                         stat2[:, 32 + 8 * s:32 + 8 * s + 4], axis=AX)
                            v.drain()
                            if g >= 4:
                                v.wait_ge(Z["ssq2"], g - 3)
                            v.tensor_scalar(stat2[:, 8 * s + 6:8 * s + 7],
                                            stat2[:, 8 * s + 5:8 * s + 6],
                                            1.0 / C, LN_EPS, ALU.mult, ALU.add)\
                                .then_inc(Z["sv12"], 1)
                            v.wait_ge(Z["ssq2"], g + 1)
                            v.reciprocal(stat2[:, 8 * s + 7:8 * s + 8],
                                         stat2[:, 8 * s + 6:8 * s + 7])
                            v.drain()
                            v.tensor_tensor(stat2[:, 8 * s + 5:8 * s + 6],
                                            stat2[:, 8 * s + 4:8 * s + 5],
                                            stat2[:, 8 * s + 7:8 * s + 8], ALU.mult)\
                                .then_inc(Z["ami"], 1)
                            v.wait_ge(Z["axp"], g + 1)
                            if g >= 2:
                                v.wait_ge(Z[f"ast_{g % 2}"], 16 * (g // 2))
                            for n in range(4):
                                sttm = v.scalar_tensor_tensor(
                                    x1bf[g % 2][:, 512 * n:512 * (n + 1)],
                                    PS[4 + n][0:128, 0:512],
                                    stat2[:, 8 * s + 7:8 * s + 8],
                                    x0p[g % 2][:, 512 * n:512 * (n + 1)],
                                    ALU.mult, ALU.add)
                            sttm.then_inc(Z["aep"], 1)

                        for g in range(NG + 1):
                            if g < NG:
                                emit_div(g)
                            if g >= 1:
                                emit_epi(g - 1)

                    @blk.scalar
                    def _(sc):
                        NG = NT * NTC

                        def emit_muls(g):
                            sc.wait_ge(Z["adc"], g + 1)
                            if g >= 2:
                                sc.wait_ge(Z["atp"], 16 * (g - 1))
                            for j in range(16):
                                for a in range(4):
                                    mu = sc.activation(
                                        attn_tm[g % 2][:, 128 * j + 32 * a:128 * j + 32 * (a + 1)],
                                        araw[g % 2][:, 136 * j + 34 * a:136 * j + 34 * a + 32],
                                        AF.Copy, scale=recb[g % 2][:, 4 * j + a:4 * j + a + 1])
                                mu.then_inc(Z["adv"], 1)

                        def emit_ln2(g):
                            s = g % 4
                            sc.wait_ge(Z["snm2"], g + 1)
                            if g >= 2:
                                sc.wait_ge(Z["sva2"], g - 1)
                            for n in range(4):
                                va = sc.activation(dump2[g % 2][:, 512 * n:512 * (n + 1)],
                                                   PS[4 + n][0:128, 0:512], AF.Square,
                                                   bias=stat2[:, 8 * s + 4:8 * s + 5],
                                                   accum_out=stat2[:, 32 + 8 * s + n:32 + 8 * s + n + 1])
                            va.then_inc(Z["sva2"], 1)
                            sc.wait_ge(Z["sv12"], g + 1)
                            sc.activation(stat2[:, 8 * s + 6:8 * s + 7],
                                          stat2[:, 8 * s + 6:8 * s + 7], AF.Sqrt)\
                                .then_inc(Z["ssq2"], 1)

                        for g in range(NG + 1):
                            if g < NG:
                                emit_muls(g)
                            if g >= 1:
                                emit_ln2(g - 1)

        es_wp.close()

        if "p2b" in phases:
            NHC = HFF // 128
            with contextlib.ExitStack() as pb:
                EB = pb.enter_context
                x1ld = [EB(nc.sbuf_tensor(f"b_x1l{i}", [128, C], BF16)) for i in range(4)]
                xn3tm = [EB(nc.sbuf_tensor(f"b_xn3t{i}", [128, C], BF16)) for i in range(2)]
                xn3fm = EB(nc.sbuf_tensor("b_xn3f", [128, NCC * 512], BF16))
                stat3 = EB(nc.sbuf_tensor("b_stat3", [128, 32], F32))
                stat4 = EB(nc.sbuf_tensor("b_stat4", [128, 32], F32))
                dump3 = EB(nc.sbuf_tensor("b_dmp", [128, C], F32))
                wgsl = [EB(nc.sbuf_tensor(f"b_wg{i}", [128, C], BF16)) for i in range(2)]
                wusl = [EB(nc.sbuf_tensor(f"b_wu{i}", [128, C], BF16)) for i in range(2)]
                sgb = [EB(nc.sbuf_tensor(f"b_sg{i}", [128, 512], BF16)) for i in range(2)]
                hfull = EB(nc.sbuf_tensor("b_h", [128, NHC * 512], BF16))
                hsqb = [EB(nc.sbuf_tensor(f"b_hsq{i}", [128, 512], BF16)) for i in range(2)]
                wpsl = [EB(nc.sbuf_tensor(f"b_wp{i}", [128, NHC * 128], BF16)) for i in range(2)]
                srow = EB(nc.sbuf_tensor("b_srow", [1, 512], F32))
                epsr = EB(nc.sbuf_tensor("b_epsr", [1, 1], F32))
                invbc = EB(nc.sbuf_tensor("b_inv", [128, 512], F32))
                zfm = [EB(nc.sbuf_tensor(f"b_zfm{i}", [128, 512], BF16)) for i in range(2)]
                ztm = EB(nc.sbuf_tensor("b_ztm", [128, NTC * 2048], BF16))
                ybuf = [EB(nc.sbuf_tensor(f"b_y{i}", [128, C], F32)) for i in range(2)]

                with nc.Block() as blk:
                    @blk.vector
                    def _(v):
                        v.memset(epsr[:], RMS_EPS)
                with nc.Block() as blk:
                    @blk.sync
                    def _(sy):
                        for t in range(NT):
                            for tc in range(NTC):
                                g = NTC * t + tc
                                if g >= 4:
                                    sy.wait_ge(Z["bep"], g - 3)
                                sy.dma_start(x1ld[g % 4][:], X1SC.ap()[128 * g:128 * (g + 1), :])\
                                    .then_inc(Z[f"bx1_{g % 4}"], 16)
                            for hc in range(NHC):
                                hg = NHC * t + hc
                                if hg >= 2:
                                    sy.wait_ge(Z["bgm"], hg - 1)
                                    sy.wait_ge(Z["bum"], hg - 1)
                                if t == 0 and hc == 0:
                                    sy.wait_ge(Z["ccq"], 1)
                                sy.dma_start(wgsl[hc % 2][:].rearrange("p (c k) -> p c k", c=NCC),
                                             qall_wgu(hc))\
                                    .then_inc(Z[f"bwg_{hc % 2}"], 16)
                                sy.dma_start(wusl[hc % 2][:].rearrange("p (c k) -> p c k", c=NCC),
                                             qall_wgu(64 + hc))\
                                    .then_inc(Z[f"bwg_{hc % 2}"], 16)
                            for co2 in range(16):
                                cg = 16 * t + co2
                                if cg >= 2:
                                    sy.wait_ge(Z["bfm"], cg - 1)
                                sy.dma_start(wpsl[co2 % 2][:].rearrange("p (c k) -> p c k", c=NHC),
                                             qall_wp(co2))\
                                    .then_inc(Z[f"bwp_{co2 % 2}"], 16)
                            for tc in range(NTC):
                                g = NTC * t + tc
                                sy.wait_ge(Z["bep"], g + 1)
                                sy.dma_start(Y.ap()[128 * g:128 * (g + 1), :], ybuf[g % 2][:])\
                                    .then_inc(Z[f"yst_{g % 2}"], 16)
                        sy.wait_ge(Z["yst_0"], 16 * 2 * NT)
                        sy.wait_ge(Z["yst_1"], 16 * 2 * NT)

                    @blk.vector
                    def _(v):
                        for t in range(NT):
                            for tc in range(NTC):
                                g = NTC * t + tc
                                s = g % 4
                                v.wait_ge(Z[f"bx1_{g % 4}"], 16 * (g // 4 + 1))
                                if t >= 1 and tc == 0:
                                    v.wait_ge(Z["sva2"], 32 + 8 * t)
                                v.reduce_sum(stat3[:, 8 * s:8 * s + 1], x1ld[g % 4][:], axis=AX)
                            v.drain()
                            for tc in range(NTC):
                                g = NTC * t + tc
                                s = g % 4
                                v.tensor_scalar(stat3[:, 8 * s:8 * s + 1], stat3[:, 8 * s:8 * s + 1],
                                                -1.0 / C, None, ALU.mult).then_inc(Z["snm2"], 1)
                            for tc in range(NTC):
                                g = NTC * t + tc
                                s = g % 4
                                v.wait_ge(Z["sva2"], 32 + 8 * t + tc + 1)
                                v.tensor_scalar(stat3[:, 8 * s + 2:8 * s + 3],
                                                stat3[:, 8 * s + 1:8 * s + 2],
                                                1.0 / C, LN_EPS, ALU.mult, ALU.add)\
                                    .then_inc(Z["sv12"], 1)
                            for tc in range(NTC):
                                g = NTC * t + tc
                                s = g % 4
                                v.wait_ge(Z["ssq2"], 32 + 8 * t + tc + 1)
                                v.reciprocal(stat3[:, 8 * s + 4:8 * s + 5],
                                             stat3[:, 8 * s + 3:8 * s + 4])
                            v.drain()
                            for tc in range(NTC):
                                g = NTC * t + tc
                                s = g % 4
                                if g >= 2:
                                    v.wait_ge(Z["btp"], 16 * (g - 1))
                                v.tensor_scalar(xn3tm[g % 2][:], x1ld[g % 4][:],
                                                stat3[:, 8 * s:8 * s + 1],
                                                stat3[:, 8 * s + 4:8 * s + 5],
                                                ALU.add, ALU.mult).then_inc(Z["bxn"], 1)
                                for cc in range(NCC):
                                    gt = 16 * g + cc
                                    v.wait_ge(Z["btp"], gt + 1)
                                    v.tensor_copy(xn3fm[:, 512 * cc + 128 * tc:512 * cc + 128 * (tc + 1)],
                                                  psbf(5 + gt % 2)).then_inc(Z["btpe"], 1)
                            for hc in range(NHC):
                                hg = NHC * t + hc
                                v.wait_ge(Z["bum"], hg + 1)
                                v.wait_ge(Z["bsg"], hg + 1)
                                if hc == 0:
                                    v.wait_ge(Z["bfm"], 16 * t)
                                v.tensor_tensor(hfull[:, 512 * hc:512 * (hc + 1)],
                                                PS[1][0:128, 0:512], sgb[hc % 2][:], ALU.mult)\
                                    .then_inc(Z["bhm"], 1)
                            v.wait_ge(Z["brw"], t + 1)
                            if t >= 1:
                                v.wait_ge(Z["bz1"], 16 * t)
                            v.reciprocal(invbc[0:1, 0:512], srow[0:1, :]).then_inc(Z["bvr"], 1)
                            v.wait_ge(Z["bvb"], t + 1)
                            v.tensor_copy(invbc[0:128, 0:512], PS[7][0:128, 0:512])\
                                .then_inc(Z["biv"], 1)
                            v.drain()
                            for co2 in range(16):
                                cg = 16 * t + co2
                                if co2 == 0 and t >= 1:
                                    v.wait_ge(Z["bep"], 4 * t)
                                v.wait_ge(Z["bfm"], cg + 1)
                                v.wait_ge(Z["biv"], t + 1)
                                if cg >= 2:
                                    v.wait_ge(Z["bzt"], 4 * (cg - 1))
                                v.tensor_tensor(zfm[co2 % 2][:], PS[3 + cg % 2][0:128, 0:512],
                                                invbc[:, 0:512], ALU.mult).then_inc(Z["bz1"], 1)
                                for tc in range(NTC):
                                    zt = 4 * cg + tc
                                    v.wait_ge(Z["bzt"], zt + 1)
                                    v.tensor_copy(ztm[:, 2048 * tc + 128 * co2:2048 * tc + 128 * (co2 + 1)],
                                                  psbf(5 + zt % 2)).then_inc(Z["bzte"], 1)
                            for tc in range(NTC):
                                g = NTC * t + tc
                                s = g % 4
                                if tc == 0:
                                    v.wait_ge(Z["bzte"], 64 * (t + 1))
                                v.reduce_sum(stat4[:, 8 * s:8 * s + 1],
                                             ztm[:, 2048 * tc:2048 * (tc + 1)], axis=AX)
                            v.drain()
                            for tc in range(NTC):
                                g = NTC * t + tc
                                s = g % 4
                                v.tensor_scalar(stat4[:, 8 * s:8 * s + 1], stat4[:, 8 * s:8 * s + 1],
                                                -1.0 / C, None, ALU.mult).then_inc(Z["snm2"], 1)
                            for tc in range(NTC):
                                g = NTC * t + tc
                                s = g % 4
                                v.wait_ge(Z["sva2"], 32 + 8 * t + 4 + tc + 1)
                                v.tensor_scalar(stat4[:, 8 * s + 2:8 * s + 3],
                                                stat4[:, 8 * s + 1:8 * s + 2],
                                                1.0 / C, LN_EPS, ALU.mult, ALU.add)\
                                    .then_inc(Z["sv12"], 1)
                            for tc in range(NTC):
                                g = NTC * t + tc
                                s = g % 4
                                v.wait_ge(Z["ssq2"], 32 + 8 * t + 4 + tc + 1)
                                v.reciprocal(stat4[:, 8 * s + 4:8 * s + 5],
                                             stat4[:, 8 * s + 3:8 * s + 4])
                            v.drain()
                            for half in range(2):
                                for tc in (2 * half, 2 * half + 1):
                                    g = NTC * t + tc
                                    s = g % 4
                                    if g >= 2:
                                        v.wait_ge(Z[f"yst_{g % 2}"], 16 * (g // 2))
                                    v.tensor_scalar(ybuf[g % 2][:], ztm[:, 2048 * tc:2048 * (tc + 1)],
                                                    stat4[:, 8 * s:8 * s + 1],
                                                    stat4[:, 8 * s + 4:8 * s + 5],
                                                    ALU.add, ALU.mult)
                                v.drain()
                                for tc in (2 * half, 2 * half + 1):
                                    g = NTC * t + tc
                                    v.tensor_tensor(ybuf[g % 2][:], ybuf[g % 2][:], x1ld[g % 4][:],
                                                    ALU.add).then_inc(Z["bep"], 1)
                                v.drain()

                    @blk.tensor
                    def _(te):
                        for t in range(NT):
                            for tc in range(NTC):
                                g = NTC * t + tc
                                for cc in range(NCC):
                                    gt = 16 * g + cc
                                    if cc == 0:
                                        te.wait_ge(Z["bxn"], g + 1)
                                    if gt >= 2:
                                        te.wait_ge(Z["btpe"], gt - 1)
                                    if t > 0 and tc == 0 and cc == 0:
                                        te.wait_ge(Z["bzte"], 64 * t)
                                    te.transpose(psbf(5 + gt % 2),
                                                 xn3tm[g % 2][:, 128 * cc:128 * (cc + 1)],
                                                 eye_bf[:]).then_inc(Z["btp"], 1)
                            def emit_hsq(hc):
                                hg = NHC * t + hc
                                te.wait_ge(Z["bpq"], hg + 1)
                                if hc == 0:
                                    te.wait_ge(Z["brw"], t)
                                te.matmul(PS[2][0:1, 0:512], onesc_bf[:], hsqb[hc % 2][:],
                                          start=(hc == 0), stop=(hc == NHC - 1))\
                                    .then_inc(Z["bhq"], 1)
                            for hc in range(NHC):
                                hg = NHC * t + hc
                                te.wait_ge(Z[f"bwg_{hc % 2}"], 32 * (hg // 2 + 1))
                                if hc == 0:
                                    te.wait_ge(Z["btpe"], 64 * (t + 1))
                                if hg >= 1:
                                    te.wait_ge(Z["bsg"], hg)
                                for ci in range(NCC):
                                    m = te.matmul(PS[0][0:128, 0:512],
                                                  wgsl[hc % 2][:, 128 * ci:128 * (ci + 1)],
                                                  xn3fm[:, 512 * ci:512 * (ci + 1)],
                                                  start=(ci == 0), stop=(ci == NCC - 1))
                                m.then_inc(Z["bgm"], 1)
                                if hg >= 1:
                                    te.wait_ge(Z["bhm"], hg)
                                for ci in range(NCC):
                                    m = te.matmul(PS[1][0:128, 0:512],
                                                  wusl[hc % 2][:, 128 * ci:128 * (ci + 1)],
                                                  xn3fm[:, 512 * ci:512 * (ci + 1)],
                                                  start=(ci == 0), stop=(ci == NCC - 1))
                                m.then_inc(Z["bum"], 1)
                                if hc >= 1:
                                    emit_hsq(hc - 1)
                            emit_hsq(NHC - 1)
                            te.wait_ge(Z["bvr"], t + 1)
                            if t >= 1:
                                te.wait_ge(Z["biv"], t)
                            te.matmul(PS[7][0:128, 0:512], onesr_f32[0:1, :],
                                      invbc[0:1, 0:512], start=True, stop=True)\
                                .then_inc(Z["bvb"], 1)
                            def emit_ztr(co2):
                                cg = 16 * t + co2
                                for tc in range(NTC):
                                    zt = 4 * cg + tc
                                    te.wait_ge(Z["bz1"], cg + 1)
                                    if zt >= 2:
                                        te.wait_ge(Z["bzte"], zt - 1)
                                    te.transpose(psbf(5 + zt % 2),
                                                 zfm[co2 % 2][:, 128 * tc:128 * (tc + 1)],
                                                 eye_bf[:]).then_inc(Z["bzt"], 1)
                            for co2 in range(16):
                                cg = 16 * t + co2
                                te.wait_ge(Z[f"bwp_{co2 % 2}"], 16 * (cg // 2 + 1))
                                if cg >= 2:
                                    te.wait_ge(Z["bz1"], cg - 1)
                                for hc in range(NHC):
                                    m = te.matmul(PS[3 + cg % 2][0:128, 0:512],
                                                  wpsl[co2 % 2][:, 128 * hc:128 * (hc + 1)],
                                                  hfull[:, 512 * hc:512 * (hc + 1)],
                                                  start=(hc == 0), stop=(hc == NHC - 1))
                                m.then_inc(Z["bfm"], 1)
                                if co2 >= 1:
                                    emit_ztr(co2 - 1)
                            emit_ztr(15)

                    @blk.scalar
                    def _(sc):
                        for t in range(NT):
                            for tc in range(NTC):
                                g = NTC * t + tc
                                s = g % 4
                                sc.wait_ge(Z["snm2"], 32 + 8 * t + tc + 1)
                                if not (t == 0 and tc == 0):
                                    sc.wait_ge(Z["sva2"], 32 + 8 * t + tc)
                                sc.activation(dump3[:], x1ld[g % 4][:], AF.Square,
                                              bias=stat3[:, 8 * s:8 * s + 1],
                                              accum_out=stat3[:, 8 * s + 1:8 * s + 2])\
                                    .then_inc(Z["sva2"], 1)
                                sc.wait_ge(Z["sv12"], 32 + 8 * t + tc + 1)
                                sc.activation(stat3[:, 8 * s + 3:8 * s + 4],
                                              stat3[:, 8 * s + 2:8 * s + 3], AF.Sqrt)\
                                    .then_inc(Z["ssq2"], 1)
                            for hc in range(NHC):
                                hg = NHC * t + hc
                                sc.wait_ge(Z["bgm"], hg + 1)
                                if hg >= 2:
                                    sc.wait_ge(Z["bhm"], hg - 1)
                                sc.activation(sgb[hc % 2][:], PS[0][0:128, 0:512], AF.Silu)\
                                    .then_inc(Z["bsg"], 1)
                            sc.wait_ge(Z["bhq"], NHC * (t + 1))
                            sc.activation(srow[0:1, :], PS[2][0:1, 0:512], AF.Sqrt,
                                          bias=epsr[0:1, 0:1], scale=1.0 / HFF)\
                                .then_inc(Z["brw"], 1)
                            for tc in range(NTC):
                                g = NTC * t + tc
                                s = g % 4
                                sc.wait_ge(Z["snm2"], 32 + 8 * t + 4 + tc + 1)
                                sc.wait_ge(Z["sva2"], 32 + 8 * t + 4 + tc)
                                sc.activation(dump3[:], ztm[:, 2048 * tc:2048 * (tc + 1)],
                                              AF.Square, bias=stat4[:, 8 * s:8 * s + 1],
                                              accum_out=stat4[:, 8 * s + 1:8 * s + 2])\
                                    .then_inc(Z["sva2"], 1)
                                sc.wait_ge(Z["sv12"], 32 + 8 * t + 4 + tc + 1)
                                sc.activation(stat4[:, 8 * s + 3:8 * s + 4],
                                              stat4[:, 8 * s + 2:8 * s + 3], AF.Sqrt)\
                                    .then_inc(Z["ssq2"], 1)

                    @blk.gpsimd
                    def _(gp):
                        for t in range(NT):
                            for hc in range(NHC):
                                hg = NHC * t + hc
                                gp.wait_ge(Z["bhm"], hg + 1)
                                if hg >= 2:
                                    gp.wait_ge(Z["bhq"], hg - 1)
                                gp.tensor_tensor(hsqb[hc % 2][:],
                                                 hfull[:, 512 * hc:512 * (hc + 1)],
                                                 hfull[:, 512 * hc:512 * (hc + 1)], ALU.mult)\
                                    .then_inc(Z["bpq"], 1)

        if dbg:
            with nc.Block() as blk:
                @blk.sync
                def _(sy):
                    n = 0
                    pairs = [("dbg_vkbd", vkbd_bf[:])]
                    for name, src in pairs:
                        if name in dbg:
                            sy.dma_start(dbg[name].ap(), src).then_inc(Z["dbgs"], 16)
                            n += 16
                    dram_pairs = [("dbg_q", QSC), ("dbg_x1", X1SC), ("dbg_qall", QALL)]
                    for name, src in dram_pairs:
                        if name in dbg:
                            sy.dma_start(dbg[name].ap(), src.ap()).then_inc(Z["dbgs"], 16)
                            n += 16
                    if "dbg_wguq" in dbg:
                        sy.dma_start(dbg["dbg_wguq"].ap(), WGUQ.ap()).then_inc(Z["dbgs"], 16)
                        n += 16
                    if n:
                        sy.wait_ge(Z["dbgs"], n)

    nc.finalize()
    return nc


def _prep_inputs(L0_attn_x, L0_attn_w_qkv, L0_attn_w_proj, L0_attn_b_proj,
                 L0_ffn_w_gate_up, L0_ffn_w_proj):
    ca = np.ascontiguousarray
    wq = ca(L0_attn_w_qkv[0:C].T)
    wk = ca(L0_attn_w_qkv[C:2 * C].T)
    wv = ca(L0_attn_w_qkv[2 * C:3 * C].T)
    wproj = ca(L0_attn_w_proj.T)
    bproj = ca(np.asarray(L0_attn_b_proj).reshape(1, C))
    wgu_fm = np.asarray(L0_ffn_w_gate_up).T
    wp_fm = np.asarray(L0_ffn_w_proj).T
    eyeb = np.eye(128).astype(ml_dtypes.bfloat16)
    g2 = np.zeros((2, 128), np.float32)
    g2[0, 0:64] = 1.0
    g2[1, 64:128] = 1.0
    g2 = g2.astype(ml_dtypes.bfloat16)
    gmat = np.zeros((128, 2), np.float32)
    gmat[0:64, 0] = 1.0
    gmat[64:128, 1] = 1.0
    gmat = gmat.astype(ml_dtypes.bfloat16)
    in_maps = []
    for c in range(NCORES):
        b, h = c // 2, c % 2
        xs = ca(np.asarray(L0_attn_x)[b, h * T:(h + 1) * T, :])
        in_maps.append({"x": xs, "wq": wq, "wk": wk, "wv": wv, "wproj": wproj,
                        "bproj": bproj,
                        "wgu": ca(wgu_fm[:, 2048 * c:2048 * (c + 1)]),
                        "wp": ca(wp_fm[:, 256 * c:256 * (c + 1)]),
                        "eyeb": eyeb, "g2in": g2, "gin": gmat})
    return in_maps


def kernel(L0_attn_x, L0_attn_w_qkv, L0_attn_w_proj, L0_attn_b_proj,
           L0_ffn_w_gate_up, L0_ffn_w_proj):
    in_maps = _prep_inputs(L0_attn_x, L0_attn_w_qkv, L0_attn_w_proj, L0_attn_b_proj,
                           L0_ffn_w_gate_up, L0_ffn_w_proj)
    nc = build()
    res = run_bass_kernel_spmd(nc, in_maps, core_ids=list(range(NCORES)))
    out = np.empty((4, 8192, C), np.float32)
    for c in range(NCORES):
        b, h = c // 2, c % 2
        out[b, h * T:(h + 1) * T, :] = res.results[c]["y"]
    return out


# version 15
# speedup vs baseline: 1.0566x; 1.0566x over previous
import contextlib
import numpy as np
import ml_dtypes
import concourse.bass as bass
import concourse.mybir as mybir
import concourse.bass_utils as _bu
from concourse.bass_utils import run_bass_kernel_spmd

if not getattr(_bu, "_ldw_patch", False):
    _orig_run_command = _bu.run_command

    def _run_command_ldw(cmd, *a, **kw):
        try:
            cmd = [c.replace("--enable-ldw-opt=false", "--enable-ldw-opt=true")
                   if isinstance(c, str) else c for c in cmd]
        except Exception:
            pass
        return _orig_run_command(cmd, *a, **kw)

    _bu.run_command = _run_command_ldw
    _bu._ldw_patch = True

F32 = mybir.dt.float32
BF16 = mybir.dt.bfloat16
AF = mybir.ActivationFunctionType
ALU = mybir.AluOpType
AX = mybir.AxisListType.X

NCORES = 8
T = 4096
C = 2048
HFF = 8192
TILE = 512
NT = T // TILE
NTC = TILE // 128
NCC = C // 128
CQ = 0.501953125
INV_CQ = 1.0 / CQ
LN_EPS = 1e-5
RMS_EPS = float(np.finfo(np.float32).eps)
DEN_EPS = 1e-15

PHASE_ALL = ("p1", "cc", "qnt", "p2a", "p2b")


def build(phases=PHASE_ALL, debug=()):
    nc = bass.Bass("TRN2", target_bir_lowering=False, debug=False)

    X = nc.dram_tensor("x", [T, C], F32, kind="ExternalInput")
    WQ = nc.dram_tensor("wq", [C, C], F32, kind="ExternalInput")
    WK = nc.dram_tensor("wk", [C, C], F32, kind="ExternalInput")
    WV = nc.dram_tensor("wv", [C, C], F32, kind="ExternalInput")
    WPROJ = nc.dram_tensor("wproj", [C, C], F32, kind="ExternalInput")
    BPROJ = nc.dram_tensor("bproj", [1, C], F32, kind="ExternalInput")
    WGU = nc.dram_tensor("wgu", [C, 2048], F32, kind="ExternalInput")
    WP = nc.dram_tensor("wp", [HFF, 256], F32, kind="ExternalInput")
    EYEB = nc.dram_tensor("eyeb", [128, 128], BF16, kind="ExternalInput")
    G2IN = nc.dram_tensor("g2in", [2, 128], BF16, kind="ExternalInput")
    GIN = nc.dram_tensor("gin", [128, 2], BF16, kind="ExternalInput")
    Y = nc.dram_tensor("y", [T, C], F32, kind="ExternalOutput")

    QSC = nc.dram_tensor("q_sc", [C, T], BF16)
    X1SC = nc.dram_tensor("x1_sc", [T, C], BF16)
    WGU_PART = 16 * 128 * NCC * 128
    WP_PART = 2 * 128 * 64 * 128
    SHARE = WGU_PART + WP_PART
    QPART = nc.dram_tensor("qpart", [SHARE], BF16)
    QALL = nc.dram_tensor("qall", [8 * SHARE], BF16)

    def qall_wgu(hc):
        rank, loc = hc // 16, hc % 16
        off = rank * SHARE + loc * (128 * NCC * 128)
        return QALL.ap()[off:off + 128 * NCC * 128]\
            .rearrange("(p c k) -> p c k", p=128, c=NCC)

    def qall_wp(co2):
        rank, loc = co2 // 2, co2 % 2
        off = rank * SHARE + WGU_PART + loc * (128 * 64 * 128)
        return QALL.ap()[off:off + 128 * 64 * 128]\
            .rearrange("(p c k) -> p c k", p=128, c=64)
    CCIN = nc.dram_tensor("cc_in", [128, 2176], F32)
    CCOUT = nc.dram_tensor("cc_out", [128, 2176], F32)

    dbg = {}
    for name, shape, dt in debug:
        dbg[name] = nc.dram_tensor(name, shape, dt, kind="ExternalOutput")

    st = contextlib.ExitStack()
    with st:
        E = st.enter_context
        eye_bf = E(nc.sbuf_tensor("eye_bf", [128, 128], BF16))
        vkbd_bf = E(nc.sbuf_tensor("vkbd_bf", [128, 2176], BF16))
        g2_bf = E(nc.sbuf_tensor("g2_bf", [2, 128], BF16))
        g_bf = E(nc.sbuf_tensor("g_bf", [128, 2], BF16))
        onesc_bf = E(nc.sbuf_tensor("onesc_bf", [128, 1], BF16))
        onesr_f32 = E(nc.sbuf_tensor("onesr_f32", [1, 128], F32))
        PS = [E(nc.psum_tensor(f"ps{i}", [128, 512], F32)) for i in range(8)]

        semnames = [
            "sx0_0", "sx0_1", "sx0_2", "sx0_3", "sLN", "snm", "sva", "sv1",
            "ssq", "stp", "stpe",
            "swq_0", "swq_1", "swq_2", "swq_3", "sqmm", "sqev",
            "sqst_0", "sqst_1", "swkv_0", "swkv_1", "skmm", "svmm",
            "skev", "svev", "svkg", "svka",
            "ccd", "ccs", "ccq",
            "qwl_0", "qwl_1", "qas", "qmm", "qsr", "qbc", "qsg", "qmk",
            "qw1", "qwh", "qst_0", "qst_1",
            "aql_0", "aql_1", "ax0_0", "ax0_1", "aoe", "adv", "adc",
            "atp", "atpe", "apj", "aep", "aden", "arw", "ast_0", "ast_1", "awp",
            "snm2", "sva2", "sv12", "ssq2",
            "bx1_0", "bx1_1", "bx1_2", "bx1_3", "btp", "btpe",
            "bwg_0", "bwg_1", "bgm", "bum", "bsg", "bhm", "bpq", "bhq",
            "bwp_0", "bwp_1", "bfm", "bz1", "bzt", "bzte", "brw", "biv",
            "bep", "yst_0", "yst_1", "bxn", "bvr", "bvb",

            "dbgs", "p0c", "sinit",
        ]
        Z = {n: E(nc.semaphore(n)) for n in semnames}

        def psbf(i, lo=0):
            return PS[i][:, 64 * lo:64 * (lo + 1)].bitcast(BF16)

        with nc.Block() as blk:
            @blk.sync
            def _(sy):
                sy.dma_start(eye_bf[:], EYEB.ap()).then_inc(Z["p0c"], 16)
                sy.dma_start(g2_bf[:], G2IN.ap()).then_inc(Z["p0c"], 16)
                sy.dma_start(g_bf[:], GIN.ap()).then_inc(Z["p0c"], 16)
                sy.wait_ge(Z["p0c"], 48)
            @blk.vector
            def _(v):
                v.memset(onesc_bf[:], 1.0)
                v.memset(onesr_f32[:], 1.0)
                v.memset(vkbd_bf[:], 0.0)

        p1cc = contextlib.ExitStack()
        if "p1" in phases or "cc" in phases:
            vk_sb = p1cc.enter_context(nc.sbuf_tensor("vk_sb", [128, 2176], F32))
            with nc.Block() as blk:
                @blk.vector
                def _(v):
                    v.memset(vk_sb[:], 0.0)
        if "p1" in phases:
            with contextlib.ExitStack() as p1:
                E1 = p1.enter_context
                x0b = [E1(nc.sbuf_tensor(f"x0b{i}", [128, C], F32)) for i in range(4)]
                xn1tm = [E1(nc.sbuf_tensor(f"xn1tm{i}", [128, C], BF16)) for i in range(2)]
                xn1fm = E1(nc.sbuf_tensor("xn1fm", [128, NCC * 512], BF16))
                dump = [E1(nc.sbuf_tensor(f"dump{i}", [128, C], F32)) for i in range(2)]
                stat = E1(nc.sbuf_tensor("stat", [128, 32], F32))
                wqsl = [E1(nc.sbuf_tensor(f"wqsl{i}", [128, C], BF16)) for i in range(4)]
                wksl = [E1(nc.sbuf_tensor(f"wksl{i}", [128, NCC * 512], BF16)) for i in range(2)]
                wvsl = [E1(nc.sbuf_tensor(f"wvsl{i}", [128, NCC * 512], BF16)) for i in range(2)]
                ksb = [E1(nc.sbuf_tensor(f"ksb{i}", [128, 512], BF16)) for i in range(2)]
                vpad = [E1(nc.sbuf_tensor(f"vpad{i}", [128, 544], BF16)) for i in range(2)]
                qob = [E1(nc.sbuf_tensor(f"qob{i}", [128, 512], BF16)) for i in range(2)]

                with nc.Block() as blk:
                    @blk.vector
                    def _(v):
                        v.memset(vpad[0][:], 0.0)
                        v.memset(vpad[1][:], 0.0).then_inc(Z["sinit"], 1)
                        v.wait_ge(Z["sinit"], 1)
                        for i in range(2):
                            for j in range(4):
                                for a in range(4):
                                    col = 136 * j + 34 * a + 32
                                    v.memset(vpad[i][:, col:col + 1], 1.0)

                with nc.Block() as blk:
                    @blk.sync
                    def _(sy):
                        for t in range(NT + 1):
                            if t < NT:
                                for c in range(NTC):
                                    g = NTC * t + c
                                    if g >= 4:
                                        sy.wait_ge(Z["sLN"], g - 3)
                                    sy.dma_start(x0b[g % 4][:],
                                                 X.ap()[128 * g:128 * (g + 1), :])\
                                        .then_inc(Z[f"sx0_{g % 4}"], 16)
                            if t >= 1:
                                tq = t - 1
                                for co in range(16):
                                    gq = 16 * tq + co
                                    sy.wait_ge(Z["sqev"], gq + 1)
                                    sy.dma_start(
                                        QSC.ap()[128 * co:128 * (co + 1), 512 * tq:512 * (tq + 1)],
                                        qob[gq % 2][:]).then_inc(Z[f"sqst_{gq % 2}"], 16)
                        sy.wait_ge(Z["sqst_0"], 16 * 8 * NT)
                        sy.wait_ge(Z["sqst_1"], 16 * 8 * NT)

                    @blk.gpsimd
                    def _(gp):
                        def kv_load(gk):
                            jj = gk % 4
                            if gk >= 2:
                                gp.wait_ge(Z["skmm"], 4 * (gk - 1))
                                gp.wait_ge(Z["svmm"], 4 * (gk - 1))
                            for (W_, dst) in ((WK, wksl), (WV, wvsl)):
                                src = W_.ap().rearrange("(c p) k -> p c k", c=NCC)[:, :, 512 * jj:512 * (jj + 1)]
                                gp.dma_start(dst[gk % 2][:].rearrange("p (c k) -> p c k", c=NCC),
                                             src).then_inc(Z[f"swkv_{gk % 2}"], 16)
                        for t in range(NT):
                            kv_load(4 * t + 0)
                            kv_load(4 * t + 1)
                            for co in range(16):
                                gq = 16 * t + co
                                if gq >= 4:
                                    gp.wait_ge(Z["sqmm"], gq - 3)
                                src = WQ.ap().rearrange("(c p) k -> p c k", c=NCC)[:, :, 128 * co:128 * (co + 1)]
                                gp.dma_start(wqsl[gq % 4][:].rearrange("p (c k) -> p c k", c=NCC),
                                             src).then_inc(Z[f"swq_{gq % 4}"], 16)
                            kv_load(4 * t + 2)
                            kv_load(4 * t + 3)

                    @blk.vector
                    def _(v):
                        for t in range(NT):
                            for c in range(NTC):
                                g = NTC * t + c
                                s = g % 4
                                v.wait_ge(Z[f"sx0_{g % 4}"], 16 * (g // 4 + 1))
                                if g >= 4:
                                    v.wait_ge(Z["sva"], g - 3)
                                v.reduce_sum(stat[:, 8 * s:8 * s + 1], x0b[g % 4][:], axis=AX)
                            v.drain()
                            for c in range(NTC):
                                g = NTC * t + c
                                s = g % 4
                                v.tensor_scalar(stat[:, 8 * s:8 * s + 1], stat[:, 8 * s:8 * s + 1],
                                                -1.0 / C, None, ALU.mult).then_inc(Z["snm"], 1)
                            for c in range(NTC):
                                g = NTC * t + c
                                s = g % 4
                                v.wait_ge(Z["sva"], g + 1)
                                if g >= 4:
                                    v.wait_ge(Z["ssq"], g - 3)
                                v.tensor_scalar(stat[:, 8 * s + 2:8 * s + 3], stat[:, 8 * s + 1:8 * s + 2],
                                                1.0 / C, LN_EPS, ALU.mult, ALU.add).then_inc(Z["sv1"], 1)
                            for c in range(NTC):
                                g = NTC * t + c
                                s = g % 4
                                v.wait_ge(Z["ssq"], g + 1)
                                v.reciprocal(stat[:, 8 * s + 4:8 * s + 5], stat[:, 8 * s + 3:8 * s + 4])
                            v.drain()
                            for c in range(NTC):
                                g = NTC * t + c
                                s = g % 4
                                if g >= 2:
                                    v.wait_ge(Z["stp"], 16 * (g - 1))
                                v.tensor_scalar(xn1tm[g % 2][:], x0b[g % 4][:],
                                                stat[:, 8 * s:8 * s + 1], stat[:, 8 * s + 4:8 * s + 5],
                                                ALU.add, ALU.mult).then_inc(Z["sLN"], 1)
                                for cc in range(NCC):
                                    gt = 16 * g + cc
                                    v.wait_ge(Z["stp"], gt + 1)
                                    v.tensor_copy(xn1fm[:, 512 * cc + 128 * c:512 * cc + 128 * (c + 1)],
                                                  psbf(gt % 4)).then_inc(Z["stpe"], 1)
                            for jj in range(4):
                                gk = 4 * t + jj
                                for tc in range(NTC):
                                    idx = 4 * gk + tc
                                    v.wait_ge(Z["svmm"], idx + 1)
                                    if idx >= 2:
                                        v.wait_ge(Z["svkg"], idx - 1)
                                    pv = PS[1 if tc % 2 == 0 else 3]
                                    src = pv[:, 0:512].rearrange("p (j a e) -> p j a e", j=4, a=4)
                                    dstv = vpad[tc % 2][:].rearrange("p (j x) -> p j x", j=4)[:, :, 0:136]\
                                        .rearrange("p j (a e) -> p j a e", a=4)[:, :, :, 0:32]
                                    v.tensor_copy(dstv, src).then_inc(Z["svev"], 1)
                                v.wait_ge(Z["svkg"], 4 * (gk + 1))
                                for j in range(4):
                                    col = 136 * (4 * jj + j)
                                    tt = v.tensor_tensor(vk_sb[:, col:col + 136], vk_sb[:, col:col + 136],
                                                         PS[4 + j][:, 0:136], ALU.add)
                                tt.then_inc(Z["svka"], 1)

                    @blk.scalar
                    def _(sc):
                        for t in range(NT):
                            for c in range(NTC):
                                g = NTC * t + c
                                s = g % 4
                                sc.wait_ge(Z["snm"], g + 1)
                                if g >= 4:
                                    sc.wait_ge(Z["sLN"], g - 3)
                                if g >= 2:
                                    sc.wait_ge(Z["sva"], g - 1)
                                sc.activation(dump[g % 2][:], x0b[g % 4][:], AF.Square,
                                              bias=stat[:, 8 * s:8 * s + 1],
                                              accum_out=stat[:, 8 * s + 1:8 * s + 2])\
                                    .then_inc(Z["sva"], 1)
                            for c in range(NTC):
                                g = NTC * t + c
                                s = g % 4
                                sc.wait_ge(Z["sv1"], g + 1)
                                sc.activation(stat[:, 8 * s + 3:8 * s + 4], stat[:, 8 * s + 2:8 * s + 3],
                                              AF.Sqrt).then_inc(Z["ssq"], 1)
                            for co in range(16):
                                gq = 16 * t + co
                                sc.wait_ge(Z["sqmm"], gq + 1)
                                if gq >= 2:
                                    sc.wait_ge(Z[f"sqst_{gq % 2}"], 16 * (gq // 2))
                                sc.activation(qob[gq % 2][:], PS[gq % 2][0:128, 0:512], AF.Relu)\
                                    .then_inc(Z["sqev"], 1)
                            for jj in range(4):
                                gk = 4 * t + jj
                                for tc in range(NTC):
                                    idx = 4 * gk + tc
                                    sc.wait_ge(Z["skmm"], idx + 1)
                                    if idx >= 2:
                                        sc.wait_ge(Z["svkg"], idx - 1)
                                    pk = PS[0 if tc % 2 == 0 else 2]
                                    sc.activation(ksb[tc % 2][:], pk[0:128, 0:512], AF.Relu)\
                                        .then_inc(Z["skev"], 1)

                    @blk.tensor
                    def _(te):
                        def emit_vk(idx):
                            gk, tc = idx // 4, idx % 4
                            jj = gk % 4
                            te.wait_ge(Z["skev"], idx + 1)
                            te.wait_ge(Z["svev"], idx + 1)
                            if tc == 0:
                                te.wait_ge(Z["svka"], gk)
                            for j in range(4):
                                m = te.matmul(PS[4 + j][:, 0:136],
                                              ksb[tc % 2][:, 128 * j:128 * (j + 1)],
                                              vpad[tc % 2][:, 136 * j:136 * j + 136],
                                              start=(tc == 0), stop=(tc == 3))
                            m.then_inc(Z["svkg"], 1)

                        for t in range(NT):
                            for c in range(NTC):
                                g = NTC * t + c
                                for cc in range(NCC):
                                    gt = 16 * g + cc
                                    if cc == 0:
                                        te.wait_ge(Z["sLN"], g + 1)
                                    if gt >= 4:
                                        te.wait_ge(Z["stpe"], gt - 3)
                                    if t > 0 and c == 0 and cc == 0:
                                        te.wait_ge(Z["skev"], 16 * t)
                                        te.wait_ge(Z["svev"], 16 * t)
                                        te.wait_ge(Z["sqev"], 16 * t)
                                    te.transpose(psbf(gt % 4),
                                                 xn1tm[g % 2][:, 128 * cc:128 * (cc + 1)],
                                                 eye_bf[:]).then_inc(Z["stp"], 1)
                            for co in range(16):
                                gq = 16 * t + co
                                te.wait_ge(Z[f"swq_{gq % 4}"], 16 * (gq // 4 + 1))
                                if gq >= 2:
                                    te.wait_ge(Z["sqev"], gq - 1)
                                if co == 0:
                                    te.wait_ge(Z["stpe"], 64 * (t + 1))
                                for ci in range(NCC):
                                    m = te.matmul(PS[gq % 2][0:128, 0:512],
                                                  wqsl[gq % 4][:, 128 * ci:128 * (ci + 1)],
                                                  xn1fm[:, 512 * ci:512 * (ci + 1)],
                                                  start=(ci == 0), stop=(ci == NCC - 1))
                                m.then_inc(Z["sqmm"], 1)
                            pend = None
                            for jj in range(4):
                                gk = 4 * t + jj
                                for tc in range(NTC):
                                    idx = 4 * gk + tc
                                    if tc == 0:
                                        te.wait_ge(Z[f"swkv_{gk % 2}"], 32 * (gk // 2 + 1))
                                    if jj == 0 and tc == 0:
                                        te.wait_ge(Z["sqev"], 16 * (t + 1))
                                    if idx >= 2:
                                        te.wait_ge(Z["skev"], idx - 1)
                                    pk = PS[0 if tc % 2 == 0 else 2]
                                    for ci in range(NCC):
                                        m = te.matmul(pk[0:128, 0:512],
                                                      xn1fm[:, 512 * ci + 128 * tc:512 * ci + 128 * (tc + 1)],
                                                      wksl[gk % 2][:, 512 * ci:512 * (ci + 1)],
                                                      start=(ci == 0), stop=(ci == NCC - 1))
                                    m.then_inc(Z["skmm"], 1)
                                    if pend is not None:
                                        emit_vk(pend)
                                    if idx >= 2:
                                        te.wait_ge(Z["svev"], idx - 1)
                                    pv = PS[1 if tc % 2 == 0 else 3]
                                    for ci in range(NCC):
                                        m = te.matmul(pv[0:128, 0:512],
                                                      xn1fm[:, 512 * ci + 128 * tc:512 * ci + 128 * (tc + 1)],
                                                      wvsl[gk % 2][:, 512 * ci:512 * (ci + 1)],
                                                      start=(ci == 0), stop=(ci == NCC - 1))
                                    m.then_inc(Z["svmm"], 1)
                                    pend = idx
                            emit_vk(pend)

        if "cc" in phases:
            with nc.Block() as blk:
                @blk.gpsimd
                def _(gp):
                    gp.dma_start(CCIN.ap(), vk_sb[:]).then_inc(Z["ccd"], 16)
                    gp.wait_ge(Z["ccd"], 16)
                    gp.collective_compute(
                        "AllReduce", ALU.add,
                        replica_groups=[[0, 1], [2, 3], [4, 5], [6, 7]],
                        ins=[CCIN.ap().opt()], outs=[CCOUT.ap().opt()],
                    ).then_inc(Z["ccs"], 1)
                    gp.wait_ge(Z["ccs"], 1)
                    gp.dma_start(vk_sb[:], CCOUT.ap()).then_inc(Z["ccd"], 16)
                    gp.wait_ge(Z["ccd"], 32)
                    for a in range(4):
                        src = vk_sb[32 * a:32 * (a + 1), :]\
                            .rearrange("p (j e) -> p j e", j=16)[:, :, 34 * a:34 * (a + 1)]
                        dst = vkbd_bf[32 * a:32 * (a + 1), :]\
                            .rearrange("p (j e) -> p j e", j=16)[:, :, 34 * a:34 * (a + 1)]
                        gp.dma_start(dst, src).then_inc(Z["ccd"], 16)
                    gp.wait_ge(Z["ccd"], 96)
        p1cc.close()


        if "qnt" in phases:
            bigs = [("wgu", i) for i in range(NCC)] + [("wp", i) for i in range(16)]
            NBIG = len(bigs)
            NSUB = 4 * NBIG

            def sub_w(k):
                return 512 if bigs[k // 4][0] == "wgu" else 256

            with contextlib.ExitStack() as qs:
                EQ = qs.enter_context
                wbf = [EQ(nc.sbuf_tensor(f"q_wbf{i}", [128, 2048], BF16)) for i in range(2)]
                aab = [EQ(nc.sbuf_tensor(f"q_aab{i}", [128, 512], BF16)) for i in range(4)]
                scr = [EQ(nc.sbuf_tensor(f"q_scr{i}", [2, 512], BF16)) for i in range(2)]
                scf = [EQ(nc.sbuf_tensor(f"q_scf{i}", [2, 512], F32)) for i in range(2)]
                msk = [EQ(nc.sbuf_tensor(f"q_msk{i}", [128, 512], F32)) for i in range(2)]
                w1b = [EQ(nc.sbuf_tensor(f"q_w1b{i}", [128, 512], F32)) for i in range(2)]
                sgnb = [EQ(nc.sbuf_tensor(f"q_sgn{i}", [128, 512], F32)) for i in range(4)]
                whb = [EQ(nc.sbuf_tensor(f"q_whb{i}", [128, 2048], BF16)) for i in range(2)]

                def q_src(B):
                    kind, i = bigs[B]
                    if kind == "wgu":
                        return WGU.ap()[128 * i:128 * (i + 1), :]
                    return WP.ap()[512 * i:512 * (i + 1), :]\
                        .rearrange("(c p) k -> p c k", c=4)

                def q_wbf_dst(B):
                    kind, i = bigs[B]
                    if kind == "wgu":
                        return wbf[B % 2][:]
                    return wbf[B % 2][:, 0:1024].rearrange("p (c k) -> p c k", c=4)

                def q_store_pairs(B):
                    kind, i = bigs[B]
                    out = []
                    if kind == "wgu":
                        full = QPART.ap()[0:WGU_PART]\
                            .rearrange("(a p c k) -> a p c k", a=16, p=128, c=NCC)
                        for h in range(2):
                            dst = full[8 * h:8 * (h + 1), :, i, :].transpose([1, 0, 2])
                            srcv = whb[B % 2][:, 1024 * h:1024 * (h + 1)]\
                                .rearrange("p (c k) -> p c k", c=8)
                            out.append((dst, srcv))
                    else:
                        full = QPART.ap()[WGU_PART:WGU_PART + WP_PART]\
                            .rearrange("(a p c k) -> a p c k", a=2, p=128, c=64)
                        for h in range(2):
                            dst = full[h, :, 4 * i:4 * (i + 1), :]
                            srcv = whb[B % 2][:, 0:1024]\
                                .rearrange("p (c a k) -> p c a k", c=4, a=2)[:, :, h, :]
                            out.append((dst, srcv))
                    return out

                with nc.Block() as blk:
                    @blk.gpsimd
                    def _(gp):
                        for B in range(NBIG + 2):
                            if B < NBIG:
                                if B >= 2:
                                    gp.wait_ge(Z["qas"], 4 * (B - 1))
                                    gp.wait_ge(Z["qsg"], 4 * (B - 1))
                                gp.dma_start(q_wbf_dst(B), q_src(B)).then_inc(Z[f"qwl_{B % 2}"], 16)
                            if B >= 2:
                                Bw = B - 2
                                wB = 512 if bigs[Bw][0] == "wgu" else 256
                                for s4 in range(4):
                                    j = 4 * Bw + s4
                                    gp.wait_ge(Z["qw1"], j + 1)
                                    gp.wait_ge(Z["qsg"], j + 1)
                                    if s4 == 0 and Bw >= 2:
                                        gp.wait_ge(Z[f"qst_{Bw % 2}"], 32 * (Bw // 2))
                                    gp.tensor_tensor(whb[Bw % 2][:, wB * s4:wB * (s4 + 1)],
                                                     w1b[j % 2][:, 0:wB], sgnb[j % 4][:, 0:wB],
                                                     ALU.mult).then_inc(Z["qwh"], 1)
                                gp.wait_ge(Z["qwh"], 4 * (Bw + 1))
                                for dst, srcv in q_store_pairs(Bw):
                                    gp.dma_start(dst, srcv).then_inc(Z[f"qst_{Bw % 2}"], 16)
                        gp.wait_ge(Z["qst_0"], 32 * ((NBIG + 1) // 2))
                        gp.wait_ge(Z["qst_1"], 32 * (NBIG // 2))

                    @blk.vector
                    def _(v):
                        for k in range(NSUB + 2):
                            if 1 <= k <= NSUB:
                                j = k - 1
                                wj = sub_w(j)
                                v.wait_ge(Z["qmm"], j + 1)
                                if j >= 2:
                                    v.wait_ge(Z["qbc"], j - 1)
                                    v.wait_ge(Z["qsr"], 2 * j - 2)
                                v.tensor_copy(scf[j % 2][:, 0:wj], PS[j % 2][0:2, 0:wj])\
                                    .then_inc(Z["qsr"], 1)
                                v.wait_ge(Z["qsr"], 2 * j + 1)
                                v.tensor_scalar(scr[j % 2][:, 0:wj], scf[j % 2][:, 0:wj],
                                                1.0 / 64.0, 1e-8, ALU.mult, ALU.max)\
                                    .then_inc(Z["qsr"], 1)
                            if k >= 2:
                                j = k - 2
                                wj = sub_w(j)
                                v.wait_ge(Z["qbc"], j + 1)
                                if j >= 2:
                                    v.wait_ge(Z["qw1"], j - 1)
                                v.scalar_tensor_tensor(msk[j % 2][:, 0:wj], aab[j % 4][:, 0:wj],
                                                       INV_CQ, PS[2 + j % 2][0:128, 0:wj],
                                                       ALU.mult, ALU.is_gt).then_inc(Z["qmk"], 1)
                                v.wait_ge(Z["qmk"], j + 1)
                                if j >= 2:
                                    v.wait_ge(Z["qwh"], j - 1)
                                v.tensor_tensor(w1b[j % 2][:, 0:wj], msk[j % 2][:, 0:wj],
                                                PS[2 + j % 2][0:128, 0:wj], ALU.mult)\
                                    .then_inc(Z["qw1"], 1)

                    @blk.tensor
                    def _(te):
                        for k in range(NSUB + 2):
                            if 1 <= k <= NSUB:
                                j = k - 1
                                wj = sub_w(j)
                                te.wait_ge(Z["qas"], j + 1)
                                if j >= 2:
                                    te.wait_ge(Z["qsr"], 2 * j - 2)
                                te.matmul(PS[j % 2][0:2, 0:wj], g_bf[:], aab[j % 4][:, 0:wj],
                                          start=True, stop=True).then_inc(Z["qmm"], 1)
                            if k >= 2:
                                j = k - 2
                                wj = sub_w(j)
                                te.wait_ge(Z["qsr"], 2 * (j + 1))
                                if j >= 2:
                                    te.wait_ge(Z["qw1"], j - 1)
                                te.matmul(PS[2 + j % 2][0:128, 0:wj], g2_bf[:], scr[j % 2][:, 0:wj],
                                          start=True, stop=True).then_inc(Z["qbc"], 1)

                    @blk.scalar
                    def _(sc):
                        for k in range(NSUB):
                            B = k // 4
                            wk = sub_w(k)
                            woff = wk * (k % 4)
                            sc.wait_ge(Z[f"qwl_{B % 2}"], 16 * (B // 2 + 1))
                            if k >= 4:
                                sc.wait_ge(Z["qwh"], k - 3)
                                sc.wait_ge(Z["qmm"], k - 3)
                                sc.wait_ge(Z["qmk"], k - 3)
                            sc.activation(aab[k % 4][:, 0:wk],
                                          wbf[B % 2][:, woff:woff + wk],
                                          AF.Abs).then_inc(Z["qas"], 1)
                            sc.activation(sgnb[k % 4][:, 0:wk],
                                          wbf[B % 2][:, woff:woff + wk],
                                          AF.Sign).then_inc(Z["qsg"], 1)

        if "p2a" in phases:
            with contextlib.ExitStack() as pa:
                EA = pa.enter_context
                wproj_sb = EA(nc.sbuf_tensor("wproj_sb", [128, NCC * 2048], BF16))
                brow_bf = EA(nc.sbuf_tensor("brow_bf", [1, C], BF16))
                onesr_bf = EA(nc.sbuf_tensor("onesr_bf", [1, 128], BF16))
                q_sb = [EA(nc.sbuf_tensor(f"a_qsb{i}", [128, NCC * 512], BF16)) for i in range(2)]
                x0b2 = [EA(nc.sbuf_tensor(f"a_x0b{i}", [128, C], F32)) for i in range(2)]
                recb = [EA(nc.sbuf_tensor(f"a_rec{i}", [128, 64], F32)) for i in range(2)]
                radd = EA(nc.sbuf_tensor("a_radd", [128, 64], F32))
                attn_tm = [EA(nc.sbuf_tensor(f"a_atm{i}", [128, C], BF16)) for i in range(2)]
                attn_fm = [EA(nc.sbuf_tensor(f"a_afm{i}", [128, C], BF16)) for i in range(2)]
                araw = [EA(nc.sbuf_tensor(f"a_raw{i}", [128, 2176], BF16)) for i in range(2)]
                x0p = [EA(nc.sbuf_tensor(f"a_x0p{i}", [128, C], F32)) for i in range(2)]
                dump2 = [EA(nc.sbuf_tensor(f"a_dmp{i}", [128, C], F32)) for i in range(2)]
                stat2 = EA(nc.sbuf_tensor("a_stat", [128, 64], F32))
                x1bf = [EA(nc.sbuf_tensor(f"a_x1b{i}", [128, C], BF16)) for i in range(2)]

                with nc.Block() as blk:
                    @blk.vector
                    def _(v):
                        v.memset(onesr_bf[:], 1.0)
                    @blk.gpsimd
                    def _(gp):
                        gp.collective_compute(
                            "AllGather", ALU.bypass,
                            replica_groups=[[0, 1, 2, 3, 4, 5, 6, 7]],
                            ins=[QPART.ap().rearrange("(r x) -> r x", r=128).opt()],
                            outs=[QALL.ap().rearrange("(r x) -> r x", r=1024).opt()],
                        ).then_inc(Z["ccq"], 1)
                        gp.dma_start(brow_bf[:], BPROJ.ap()).then_inc(Z["awp"], 16)
                        gp.dma_start(wproj_sb[:].rearrange("p (c k) -> p c k", c=NCC),
                                     WPROJ.ap().rearrange("(c p) k -> p c k", c=NCC))\
                            .then_inc(Z["awp"], 16)

                    @blk.sync
                    def _(sy):
                        for t in range(NT + 1):
                            if t < NT:
                                if t >= 2:
                                    sy.wait_ge(Z["aoe"], 64 * (t - 1))
                                sy.dma_start(q_sb[t % 2][:].rearrange("p (c k) -> p c k", c=NCC),
                                             QSC.ap().rearrange("(c p) k -> p c k", c=NCC)
                                             [:, :, 512 * t:512 * (t + 1)])\
                                    .then_inc(Z[f"aql_{t % 2}"], 16)
                            if t >= 1:
                                for tc in range(NTC):
                                    g = NTC * (t - 1) + tc
                                    sy.wait_ge(Z["aep"], g + 1)
                                    sy.dma_start(X1SC.ap()[128 * g:128 * (g + 1), :],
                                                 x1bf[g % 2][:]).then_inc(Z[f"ast_{g % 2}"], 16)
                            if t < NT:
                                for tc in range(NTC):
                                    g = NTC * t + tc
                                    if g >= 2:
                                        sy.wait_ge(Z["aep"], g - 1)
                                    sy.dma_start(x0b2[g % 2][:], X.ap()[128 * g:128 * (g + 1), :])\
                                        .then_inc(Z[f"ax0_{g % 2}"], 16)
                        sy.wait_ge(Z["ast_0"], 16 * 2 * NT)
                        sy.wait_ge(Z["ast_1"], 16 * 2 * NT)

                    @blk.tensor
                    def _(te):
                        NG = NT * NTC

                        def emit_oe(g):
                            t = g // NTC
                            tc = g % NTC
                            for j in range(16):
                                o = 16 * g + j
                                if j == 0 and tc == 0:
                                    te.wait_ge(Z[f"aql_{t % 2}"], 16 * (t // 2 + 1))
                                if j == 0 and g >= 2:
                                    te.wait_ge(Z["atpe"], 16 * (g - 1))
                                if o >= 3:
                                    te.wait_ge(Z["arw"], o - 2)
                                if j == 0 and g >= 1:
                                    te.wait_ge(Z["adc"], g)
                                qchunk = q_sb[t % 2][:, 512 * j + 128 * tc:512 * j + 128 * (tc + 1)]
                                te.matmul(PS[o % 3][0:128, 0:136], qchunk,
                                          vkbd_bf[:, 136 * j:136 * (j + 1)],
                                          start=True, stop=True).then_inc(Z["aoe"], 1)
                                dencols = vkbd_bf[:, 136 * j:136 * (j + 1)]\
                                    .rearrange("p (a e) -> p a e", a=4)[:, :, 32:33]\
                                    .rearrange("p a e -> p (a e)")
                                dm = te.matmul(PS[3][0:128, 4 * j:4 * (j + 1)], qchunk,
                                               dencols, start=True, stop=True)
                            dm.then_inc(Z["aden"], 1)

                        def emit_tail(g):
                            for cc in range(NCC):
                                gt = 16 * g + cc
                                if cc == 0:
                                    te.wait_ge(Z["adv"], 16 * (g + 1))
                                    te.wait_ge(Z["arw"], 16 * min(g + 2, NG))
                                if gt >= 3:
                                    te.wait_ge(Z["atpe"], gt - 2)
                                te.transpose(psbf(gt % 3),
                                             attn_tm[g % 2][:, 128 * cc:128 * (cc + 1)],
                                             eye_bf[:]).then_inc(Z["atp"], 1)
                            te.wait_ge(Z["atpe"], 16 * (g + 1))
                            if g == 0:
                                te.wait_ge(Z["awp"], 32)
                            if g >= 1:
                                te.wait_ge(Z["aep"], g)
                                te.wait_ge(Z["sva2"], g)
                            for ci in range(NCC):
                                for n in range(4):
                                    te.matmul(
                                        PS[4 + n][0:128, 0:512],
                                        attn_fm[g % 2][:, 128 * ci:128 * (ci + 1)],
                                        wproj_sb[:, 2048 * ci + 512 * n:2048 * ci + 512 * (n + 1)],
                                        start=(ci == 0), stop=False)
                            for n in range(4):
                                m = te.matmul(PS[4 + n][0:128, 0:512], onesr_bf[0:1, :],
                                              brow_bf[0:1, 512 * n:512 * (n + 1)],
                                              start=False, stop=True)
                            m.then_inc(Z["apj"], 1)

                        for g in range(NG + 1):
                            if g < NG:
                                emit_oe(g)
                            if g >= 1:
                                emit_tail(g - 1)

                    @blk.vector
                    def _(v):
                        NG = NT * NTC

                        def emit_div(g):
                            t = g // NTC
                            for j in range(16):
                                o = 16 * g + j
                                v.wait_ge(Z["aoe"], o + 1)
                                v.tensor_copy(araw[g % 2][:, 136 * j:136 * (j + 1)],
                                              PS[o % 3][0:128, 0:136]).then_inc(Z["arw"], 1)
                            v.wait_ge(Z["aden"], g + 1)
                            v.tensor_scalar(radd[:], PS[3][0:128, 0:64],
                                            DEN_EPS, None, ALU.add)
                            v.drain()
                            if g >= 2:
                                v.wait_ge(Z["adv"], 16 * (g - 1))
                            v.reciprocal(recb[g % 2][:], radd[:]).then_inc(Z["adc"], 1)

                        def emit_epi(g):
                            t = g // NTC
                            tc = g % NTC
                            s = g % 4
                            if g >= 2:
                                v.wait_ge(Z["apj"], g - 1)
                            for cc in range(NCC):
                                gt = 16 * g + cc
                                v.wait_ge(Z["atp"], gt + 1)
                                v.tensor_copy(attn_fm[g % 2][:, 128 * cc:128 * (cc + 1)],
                                              psbf(gt % 3)).then_inc(Z["atpe"], 1)
                            v.wait_ge(Z["apj"], g + 1)
                            v.wait_ge(Z[f"ax0_{g % 2}"], 16 * (g // 2 + 1))
                            for n in range(4):
                                v.reduce_sum(stat2[:, 8 * s + n:8 * s + n + 1],
                                             PS[4 + n][0:128, 0:512], axis=AX)
                            v.drain()
                            if g >= 4:
                                v.wait_ge(Z["sva2"], g - 3)
                            v.reduce_sum(stat2[:, 8 * s + 4:8 * s + 5],
                                         stat2[:, 8 * s:8 * s + 4], axis=AX)
                            v.drain()
                            v.tensor_scalar(stat2[:, 8 * s + 4:8 * s + 5],
                                            stat2[:, 8 * s + 4:8 * s + 5],
                                            -1.0 / C, None, ALU.mult).then_inc(Z["snm2"], 1)
                            v.wait_ge(Z["sva2"], g + 1)
                            v.reduce_sum(stat2[:, 8 * s + 5:8 * s + 6],
                                         stat2[:, 32 + 8 * s:32 + 8 * s + 4], axis=AX)
                            v.drain()
                            if g >= 4:
                                v.wait_ge(Z["ssq2"], g - 3)
                            v.tensor_scalar(stat2[:, 8 * s + 6:8 * s + 7],
                                            stat2[:, 8 * s + 5:8 * s + 6],
                                            1.0 / C, LN_EPS, ALU.mult, ALU.add)\
                                .then_inc(Z["sv12"], 1)
                            v.wait_ge(Z["ssq2"], g + 1)
                            v.reciprocal(stat2[:, 8 * s + 7:8 * s + 8],
                                         stat2[:, 8 * s + 6:8 * s + 7])
                            v.drain()
                            v.tensor_tensor(stat2[:, 8 * s + 5:8 * s + 6],
                                            stat2[:, 8 * s + 4:8 * s + 5],
                                            stat2[:, 8 * s + 7:8 * s + 8], ALU.mult)
                            v.drain()
                            v.tensor_scalar(x0p[g % 2][:], x0b2[g % 2][:],
                                            stat2[:, 8 * s + 5:8 * s + 6], None, ALU.add)
                            v.drain()
                            if g >= 2:
                                v.wait_ge(Z[f"ast_{g % 2}"], 16 * (g // 2))
                            for n in range(4):
                                sttm = v.scalar_tensor_tensor(
                                    x1bf[g % 2][:, 512 * n:512 * (n + 1)],
                                    PS[4 + n][0:128, 0:512],
                                    stat2[:, 8 * s + 7:8 * s + 8],
                                    x0p[g % 2][:, 512 * n:512 * (n + 1)],
                                    ALU.mult, ALU.add)
                            sttm.then_inc(Z["aep"], 1)

                        for g in range(NG + 1):
                            if g < NG:
                                emit_div(g)
                            if g >= 1:
                                emit_epi(g - 1)

                    @blk.scalar
                    def _(sc):
                        NG = NT * NTC

                        def emit_muls(g):
                            sc.wait_ge(Z["adc"], g + 1)
                            if g >= 2:
                                sc.wait_ge(Z["atp"], 16 * (g - 1))
                            for j in range(16):
                                for a in range(4):
                                    mu = sc.activation(
                                        attn_tm[g % 2][:, 128 * j + 32 * a:128 * j + 32 * (a + 1)],
                                        araw[g % 2][:, 136 * j + 34 * a:136 * j + 34 * a + 32],
                                        AF.Copy, scale=recb[g % 2][:, 4 * j + a:4 * j + a + 1])
                                mu.then_inc(Z["adv"], 1)

                        def emit_ln2(g):
                            s = g % 4
                            sc.wait_ge(Z["snm2"], g + 1)
                            if g >= 2:
                                sc.wait_ge(Z["sva2"], g - 1)
                            for n in range(4):
                                va = sc.activation(dump2[g % 2][:, 512 * n:512 * (n + 1)],
                                                   PS[4 + n][0:128, 0:512], AF.Square,
                                                   bias=stat2[:, 8 * s + 4:8 * s + 5],
                                                   accum_out=stat2[:, 32 + 8 * s + n:32 + 8 * s + n + 1])
                            va.then_inc(Z["sva2"], 1)
                            sc.wait_ge(Z["sv12"], g + 1)
                            sc.activation(stat2[:, 8 * s + 6:8 * s + 7],
                                          stat2[:, 8 * s + 6:8 * s + 7], AF.Sqrt)\
                                .then_inc(Z["ssq2"], 1)

                        for g in range(NG + 1):
                            if g < NG:
                                emit_muls(g)
                            if g >= 1:
                                emit_ln2(g - 1)

        if "p2b" in phases:
            NHC = HFF // 128
            with contextlib.ExitStack() as pb:
                EB = pb.enter_context
                x1ld = [EB(nc.sbuf_tensor(f"b_x1l{i}", [128, C], BF16)) for i in range(4)]
                xn3tm = [EB(nc.sbuf_tensor(f"b_xn3t{i}", [128, C], BF16)) for i in range(2)]
                xn3fm = EB(nc.sbuf_tensor("b_xn3f", [128, NCC * 512], BF16))
                stat3 = EB(nc.sbuf_tensor("b_stat3", [128, 32], F32))
                stat4 = EB(nc.sbuf_tensor("b_stat4", [128, 32], F32))
                dump3 = EB(nc.sbuf_tensor("b_dmp", [128, C], F32))
                wgsl = [EB(nc.sbuf_tensor(f"b_wg{i}", [128, C], BF16)) for i in range(2)]
                wusl = [EB(nc.sbuf_tensor(f"b_wu{i}", [128, C], BF16)) for i in range(2)]
                sgb = [EB(nc.sbuf_tensor(f"b_sg{i}", [128, 512], BF16)) for i in range(2)]
                hfull = EB(nc.sbuf_tensor("b_h", [128, NHC * 512], BF16))
                hsqb = [EB(nc.sbuf_tensor(f"b_hsq{i}", [128, 512], BF16)) for i in range(2)]
                wpsl = [EB(nc.sbuf_tensor(f"b_wp{i}", [128, NHC * 128], BF16)) for i in range(2)]
                srow = EB(nc.sbuf_tensor("b_srow", [1, 512], F32))
                epsr = EB(nc.sbuf_tensor("b_epsr", [1, 1], F32))
                invbc = EB(nc.sbuf_tensor("b_inv", [128, 512], F32))
                zfm = [EB(nc.sbuf_tensor(f"b_zfm{i}", [128, 512], BF16)) for i in range(2)]
                ztm = EB(nc.sbuf_tensor("b_ztm", [128, NTC * 2048], BF16))
                ybuf = [EB(nc.sbuf_tensor(f"b_y{i}", [128, C], F32)) for i in range(2)]

                with nc.Block() as blk:
                    @blk.vector
                    def _(v):
                        v.memset(epsr[:], RMS_EPS)
                with nc.Block() as blk:
                    @blk.sync
                    def _(sy):
                        for t in range(NT):
                            for tc in range(NTC):
                                g = NTC * t + tc
                                if g >= 4:
                                    sy.wait_ge(Z["bep"], g - 3)
                                sy.dma_start(x1ld[g % 4][:], X1SC.ap()[128 * g:128 * (g + 1), :])\
                                    .then_inc(Z[f"bx1_{g % 4}"], 16)
                            for hc in range(NHC):
                                hg = NHC * t + hc
                                if hg >= 2:
                                    sy.wait_ge(Z["bgm"], hg - 1)
                                    sy.wait_ge(Z["bum"], hg - 1)
                                if t == 0 and hc == 0:
                                    sy.wait_ge(Z["ccq"], 1)
                                sy.dma_start(wgsl[hc % 2][:].rearrange("p (c k) -> p c k", c=NCC),
                                             qall_wgu(hc))\
                                    .then_inc(Z[f"bwg_{hc % 2}"], 16)
                                sy.dma_start(wusl[hc % 2][:].rearrange("p (c k) -> p c k", c=NCC),
                                             qall_wgu(64 + hc))\
                                    .then_inc(Z[f"bwg_{hc % 2}"], 16)
                            for co2 in range(16):
                                cg = 16 * t + co2
                                if cg >= 2:
                                    sy.wait_ge(Z["bfm"], cg - 1)
                                sy.dma_start(wpsl[co2 % 2][:].rearrange("p (c k) -> p c k", c=NHC),
                                             qall_wp(co2))\
                                    .then_inc(Z[f"bwp_{co2 % 2}"], 16)
                            for tc in range(NTC):
                                g = NTC * t + tc
                                sy.wait_ge(Z["bep"], g + 1)
                                sy.dma_start(Y.ap()[128 * g:128 * (g + 1), :], ybuf[g % 2][:])\
                                    .then_inc(Z[f"yst_{g % 2}"], 16)
                        sy.wait_ge(Z["yst_0"], 16 * 2 * NT)
                        sy.wait_ge(Z["yst_1"], 16 * 2 * NT)

                    @blk.vector
                    def _(v):
                        for t in range(NT):
                            for tc in range(NTC):
                                g = NTC * t + tc
                                s = g % 4
                                v.wait_ge(Z[f"bx1_{g % 4}"], 16 * (g // 4 + 1))
                                if t >= 1 and tc == 0:
                                    v.wait_ge(Z["sva2"], 32 + 8 * t)
                                v.reduce_sum(stat3[:, 8 * s:8 * s + 1], x1ld[g % 4][:], axis=AX)
                            v.drain()
                            for tc in range(NTC):
                                g = NTC * t + tc
                                s = g % 4
                                v.tensor_scalar(stat3[:, 8 * s:8 * s + 1], stat3[:, 8 * s:8 * s + 1],
                                                -1.0 / C, None, ALU.mult).then_inc(Z["snm2"], 1)
                            for tc in range(NTC):
                                g = NTC * t + tc
                                s = g % 4
                                v.wait_ge(Z["sva2"], 32 + 8 * t + tc + 1)
                                v.tensor_scalar(stat3[:, 8 * s + 2:8 * s + 3],
                                                stat3[:, 8 * s + 1:8 * s + 2],
                                                1.0 / C, LN_EPS, ALU.mult, ALU.add)\
                                    .then_inc(Z["sv12"], 1)
                            for tc in range(NTC):
                                g = NTC * t + tc
                                s = g % 4
                                v.wait_ge(Z["ssq2"], 32 + 8 * t + tc + 1)
                                v.reciprocal(stat3[:, 8 * s + 4:8 * s + 5],
                                             stat3[:, 8 * s + 3:8 * s + 4])
                            v.drain()
                            for tc in range(NTC):
                                g = NTC * t + tc
                                s = g % 4
                                if g >= 2:
                                    v.wait_ge(Z["btp"], 16 * (g - 1))
                                v.tensor_scalar(xn3tm[g % 2][:], x1ld[g % 4][:],
                                                stat3[:, 8 * s:8 * s + 1],
                                                stat3[:, 8 * s + 4:8 * s + 5],
                                                ALU.add, ALU.mult).then_inc(Z["bxn"], 1)
                                for cc in range(NCC):
                                    gt = 16 * g + cc
                                    v.wait_ge(Z["btp"], gt + 1)
                                    v.tensor_copy(xn3fm[:, 512 * cc + 128 * tc:512 * cc + 128 * (tc + 1)],
                                                  psbf(5 + gt % 2)).then_inc(Z["btpe"], 1)
                            for hc in range(NHC):
                                hg = NHC * t + hc
                                v.wait_ge(Z["bum"], hg + 1)
                                v.wait_ge(Z["bsg"], hg + 1)
                                if hc == 0:
                                    v.wait_ge(Z["bfm"], 16 * t)
                                v.tensor_tensor(hfull[:, 512 * hc:512 * (hc + 1)],
                                                PS[1][0:128, 0:512], sgb[hc % 2][:], ALU.mult)\
                                    .then_inc(Z["bhm"], 1)
                            v.wait_ge(Z["brw"], t + 1)
                            if t >= 1:
                                v.wait_ge(Z["bz1"], 16 * t)
                            v.reciprocal(invbc[0:1, 0:512], srow[0:1, :]).then_inc(Z["bvr"], 1)
                            v.wait_ge(Z["bvb"], t + 1)
                            v.tensor_copy(invbc[0:128, 0:512], PS[7][0:128, 0:512])\
                                .then_inc(Z["biv"], 1)
                            v.drain()
                            for co2 in range(16):
                                cg = 16 * t + co2
                                if co2 == 0 and t >= 1:
                                    v.wait_ge(Z["bep"], 4 * t)
                                v.wait_ge(Z["bfm"], cg + 1)
                                v.wait_ge(Z["biv"], t + 1)
                                if cg >= 2:
                                    v.wait_ge(Z["bzt"], 4 * (cg - 1))
                                v.tensor_tensor(zfm[co2 % 2][:], PS[3 + cg % 2][0:128, 0:512],
                                                invbc[:, 0:512], ALU.mult).then_inc(Z["bz1"], 1)
                                for tc in range(NTC):
                                    zt = 4 * cg + tc
                                    v.wait_ge(Z["bzt"], zt + 1)
                                    v.tensor_copy(ztm[:, 2048 * tc + 128 * co2:2048 * tc + 128 * (co2 + 1)],
                                                  psbf(5 + zt % 2)).then_inc(Z["bzte"], 1)
                            for tc in range(NTC):
                                g = NTC * t + tc
                                s = g % 4
                                if tc == 0:
                                    v.wait_ge(Z["bzte"], 64 * (t + 1))
                                v.reduce_sum(stat4[:, 8 * s:8 * s + 1],
                                             ztm[:, 2048 * tc:2048 * (tc + 1)], axis=AX)
                            v.drain()
                            for tc in range(NTC):
                                g = NTC * t + tc
                                s = g % 4
                                v.tensor_scalar(stat4[:, 8 * s:8 * s + 1], stat4[:, 8 * s:8 * s + 1],
                                                -1.0 / C, None, ALU.mult).then_inc(Z["snm2"], 1)
                            for tc in range(NTC):
                                g = NTC * t + tc
                                s = g % 4
                                v.wait_ge(Z["sva2"], 32 + 8 * t + 4 + tc + 1)
                                v.tensor_scalar(stat4[:, 8 * s + 2:8 * s + 3],
                                                stat4[:, 8 * s + 1:8 * s + 2],
                                                1.0 / C, LN_EPS, ALU.mult, ALU.add)\
                                    .then_inc(Z["sv12"], 1)
                            for tc in range(NTC):
                                g = NTC * t + tc
                                s = g % 4
                                v.wait_ge(Z["ssq2"], 32 + 8 * t + 4 + tc + 1)
                                v.reciprocal(stat4[:, 8 * s + 4:8 * s + 5],
                                             stat4[:, 8 * s + 3:8 * s + 4])
                            v.drain()
                            for half in range(2):
                                for tc in (2 * half, 2 * half + 1):
                                    g = NTC * t + tc
                                    s = g % 4
                                    if g >= 2:
                                        v.wait_ge(Z[f"yst_{g % 2}"], 16 * (g // 2))
                                    v.tensor_scalar(ybuf[g % 2][:], ztm[:, 2048 * tc:2048 * (tc + 1)],
                                                    stat4[:, 8 * s:8 * s + 1],
                                                    stat4[:, 8 * s + 4:8 * s + 5],
                                                    ALU.add, ALU.mult)
                                v.drain()
                                for tc in (2 * half, 2 * half + 1):
                                    g = NTC * t + tc
                                    v.tensor_tensor(ybuf[g % 2][:], ybuf[g % 2][:], x1ld[g % 4][:],
                                                    ALU.add).then_inc(Z["bep"], 1)
                                v.drain()

                    @blk.tensor
                    def _(te):
                        for t in range(NT):
                            for tc in range(NTC):
                                g = NTC * t + tc
                                for cc in range(NCC):
                                    gt = 16 * g + cc
                                    if cc == 0:
                                        te.wait_ge(Z["bxn"], g + 1)
                                    if gt >= 2:
                                        te.wait_ge(Z["btpe"], gt - 1)
                                    if t > 0 and tc == 0 and cc == 0:
                                        te.wait_ge(Z["bzte"], 64 * t)
                                    te.transpose(psbf(5 + gt % 2),
                                                 xn3tm[g % 2][:, 128 * cc:128 * (cc + 1)],
                                                 eye_bf[:]).then_inc(Z["btp"], 1)
                            def emit_hsq(hc):
                                hg = NHC * t + hc
                                te.wait_ge(Z["bpq"], hg + 1)
                                if hc == 0:
                                    te.wait_ge(Z["brw"], t)
                                te.matmul(PS[2][0:1, 0:512], onesc_bf[:], hsqb[hc % 2][:],
                                          start=(hc == 0), stop=(hc == NHC - 1))\
                                    .then_inc(Z["bhq"], 1)
                            for hc in range(NHC):
                                hg = NHC * t + hc
                                te.wait_ge(Z[f"bwg_{hc % 2}"], 32 * (hg // 2 + 1))
                                if hc == 0:
                                    te.wait_ge(Z["btpe"], 64 * (t + 1))
                                if hg >= 1:
                                    te.wait_ge(Z["bsg"], hg)
                                for ci in range(NCC):
                                    m = te.matmul(PS[0][0:128, 0:512],
                                                  wgsl[hc % 2][:, 128 * ci:128 * (ci + 1)],
                                                  xn3fm[:, 512 * ci:512 * (ci + 1)],
                                                  start=(ci == 0), stop=(ci == NCC - 1))
                                m.then_inc(Z["bgm"], 1)
                                if hg >= 1:
                                    te.wait_ge(Z["bhm"], hg)
                                for ci in range(NCC):
                                    m = te.matmul(PS[1][0:128, 0:512],
                                                  wusl[hc % 2][:, 128 * ci:128 * (ci + 1)],
                                                  xn3fm[:, 512 * ci:512 * (ci + 1)],
                                                  start=(ci == 0), stop=(ci == NCC - 1))
                                m.then_inc(Z["bum"], 1)
                                if hc >= 1:
                                    emit_hsq(hc - 1)
                            emit_hsq(NHC - 1)
                            te.wait_ge(Z["bvr"], t + 1)
                            if t >= 1:
                                te.wait_ge(Z["biv"], t)
                            te.matmul(PS[7][0:128, 0:512], onesr_f32[0:1, :],
                                      invbc[0:1, 0:512], start=True, stop=True)\
                                .then_inc(Z["bvb"], 1)
                            def emit_ztr(co2):
                                cg = 16 * t + co2
                                for tc in range(NTC):
                                    zt = 4 * cg + tc
                                    te.wait_ge(Z["bz1"], cg + 1)
                                    if zt >= 2:
                                        te.wait_ge(Z["bzte"], zt - 1)
                                    te.transpose(psbf(5 + zt % 2),
                                                 zfm[co2 % 2][:, 128 * tc:128 * (tc + 1)],
                                                 eye_bf[:]).then_inc(Z["bzt"], 1)
                            for co2 in range(16):
                                cg = 16 * t + co2
                                te.wait_ge(Z[f"bwp_{co2 % 2}"], 16 * (cg // 2 + 1))
                                if cg >= 2:
                                    te.wait_ge(Z["bz1"], cg - 1)
                                for hc in range(NHC):
                                    m = te.matmul(PS[3 + cg % 2][0:128, 0:512],
                                                  wpsl[co2 % 2][:, 128 * hc:128 * (hc + 1)],
                                                  hfull[:, 512 * hc:512 * (hc + 1)],
                                                  start=(hc == 0), stop=(hc == NHC - 1))
                                m.then_inc(Z["bfm"], 1)
                                if co2 >= 1:
                                    emit_ztr(co2 - 1)
                            emit_ztr(15)

                    @blk.scalar
                    def _(sc):
                        for t in range(NT):
                            for tc in range(NTC):
                                g = NTC * t + tc
                                s = g % 4
                                sc.wait_ge(Z["snm2"], 32 + 8 * t + tc + 1)
                                if not (t == 0 and tc == 0):
                                    sc.wait_ge(Z["sva2"], 32 + 8 * t + tc)
                                sc.activation(dump3[:], x1ld[g % 4][:], AF.Square,
                                              bias=stat3[:, 8 * s:8 * s + 1],
                                              accum_out=stat3[:, 8 * s + 1:8 * s + 2])\
                                    .then_inc(Z["sva2"], 1)
                                sc.wait_ge(Z["sv12"], 32 + 8 * t + tc + 1)
                                sc.activation(stat3[:, 8 * s + 3:8 * s + 4],
                                              stat3[:, 8 * s + 2:8 * s + 3], AF.Sqrt)\
                                    .then_inc(Z["ssq2"], 1)
                            for hc in range(NHC):
                                hg = NHC * t + hc
                                sc.wait_ge(Z["bgm"], hg + 1)
                                if hg >= 2:
                                    sc.wait_ge(Z["bhm"], hg - 1)
                                sc.activation(sgb[hc % 2][:], PS[0][0:128, 0:512], AF.Silu)\
                                    .then_inc(Z["bsg"], 1)
                            sc.wait_ge(Z["bhq"], NHC * (t + 1))
                            sc.activation(srow[0:1, :], PS[2][0:1, 0:512], AF.Sqrt,
                                          bias=epsr[0:1, 0:1], scale=1.0 / HFF)\
                                .then_inc(Z["brw"], 1)
                            for tc in range(NTC):
                                g = NTC * t + tc
                                s = g % 4
                                sc.wait_ge(Z["snm2"], 32 + 8 * t + 4 + tc + 1)
                                sc.wait_ge(Z["sva2"], 32 + 8 * t + 4 + tc)
                                sc.activation(dump3[:], ztm[:, 2048 * tc:2048 * (tc + 1)],
                                              AF.Square, bias=stat4[:, 8 * s:8 * s + 1],
                                              accum_out=stat4[:, 8 * s + 1:8 * s + 2])\
                                    .then_inc(Z["sva2"], 1)
                                sc.wait_ge(Z["sv12"], 32 + 8 * t + 4 + tc + 1)
                                sc.activation(stat4[:, 8 * s + 3:8 * s + 4],
                                              stat4[:, 8 * s + 2:8 * s + 3], AF.Sqrt)\
                                    .then_inc(Z["ssq2"], 1)

                    @blk.gpsimd
                    def _(gp):
                        for t in range(NT):
                            for hc in range(NHC):
                                hg = NHC * t + hc
                                gp.wait_ge(Z["bhm"], hg + 1)
                                if hg >= 2:
                                    gp.wait_ge(Z["bhq"], hg - 1)
                                gp.tensor_tensor(hsqb[hc % 2][:],
                                                 hfull[:, 512 * hc:512 * (hc + 1)],
                                                 hfull[:, 512 * hc:512 * (hc + 1)], ALU.mult)\
                                    .then_inc(Z["bpq"], 1)

        if dbg:
            with nc.Block() as blk:
                @blk.sync
                def _(sy):
                    n = 0
                    pairs = [("dbg_vkbd", vkbd_bf[:])]
                    for name, src in pairs:
                        if name in dbg:
                            sy.dma_start(dbg[name].ap(), src).then_inc(Z["dbgs"], 16)
                            n += 16
                    dram_pairs = [("dbg_q", QSC), ("dbg_x1", X1SC), ("dbg_qall", QALL)]
                    for name, src in dram_pairs:
                        if name in dbg:
                            sy.dma_start(dbg[name].ap(), src.ap()).then_inc(Z["dbgs"], 16)
                            n += 16
                    if "dbg_wguq" in dbg:
                        sy.dma_start(dbg["dbg_wguq"].ap(), WGUQ.ap()).then_inc(Z["dbgs"], 16)
                        n += 16
                    if n:
                        sy.wait_ge(Z["dbgs"], n)

    nc.finalize()
    return nc


def _prep_inputs(L0_attn_x, L0_attn_w_qkv, L0_attn_w_proj, L0_attn_b_proj,
                 L0_ffn_w_gate_up, L0_ffn_w_proj):
    ca = np.ascontiguousarray
    wq = ca(L0_attn_w_qkv[0:C].T)
    wk = ca(L0_attn_w_qkv[C:2 * C].T)
    wv = ca(L0_attn_w_qkv[2 * C:3 * C].T)
    wproj = ca(L0_attn_w_proj.T)
    bproj = ca(np.asarray(L0_attn_b_proj).reshape(1, C))
    wgu_fm = np.asarray(L0_ffn_w_gate_up).T
    wp_fm = np.asarray(L0_ffn_w_proj).T
    eyeb = np.eye(128).astype(ml_dtypes.bfloat16)
    g2 = np.zeros((2, 128), np.float32)
    g2[0, 0:64] = 1.0
    g2[1, 64:128] = 1.0
    g2 = g2.astype(ml_dtypes.bfloat16)
    gmat = np.zeros((128, 2), np.float32)
    gmat[0:64, 0] = 1.0
    gmat[64:128, 1] = 1.0
    gmat = gmat.astype(ml_dtypes.bfloat16)
    in_maps = []
    for c in range(NCORES):
        b, h = c // 2, c % 2
        xs = ca(np.asarray(L0_attn_x)[b, h * T:(h + 1) * T, :])
        in_maps.append({"x": xs, "wq": wq, "wk": wk, "wv": wv, "wproj": wproj,
                        "bproj": bproj,
                        "wgu": ca(wgu_fm[:, 2048 * c:2048 * (c + 1)]),
                        "wp": ca(wp_fm[:, 256 * c:256 * (c + 1)]),
                        "eyeb": eyeb, "g2in": g2, "gin": gmat})
    return in_maps


def kernel(L0_attn_x, L0_attn_w_qkv, L0_attn_w_proj, L0_attn_b_proj,
           L0_ffn_w_gate_up, L0_ffn_w_proj):
    in_maps = _prep_inputs(L0_attn_x, L0_attn_w_qkv, L0_attn_w_proj, L0_attn_b_proj,
                           L0_ffn_w_gate_up, L0_ffn_w_proj)
    nc = build()
    res = run_bass_kernel_spmd(nc, in_maps, core_ids=list(range(NCORES)))
    out = np.empty((4, 8192, C), np.float32)
    for c in range(NCORES):
        b, h = c // 2, c % 2
        out[b, h * T:(h + 1) * T, :] = res.results[c]["y"]
    return out
